# version 15
# baseline (speedup 1.0000x reference)
import contextlib
import numpy as np
import concourse.bass as bass
import concourse.mybir as mybir
from concourse.bass_utils import run_bass_kernel_spmd

F32 = mybir.dt.float32
BF16 = mybir.dt.bfloat16
AF = mybir.ActivationFunctionType
ALU = mybir.AluOpType
ds = bass.ds

B, T, D = 4, 8192, 1024
NCORE = 8
TOKC = T // 2
TT = 256
NT = TOKC // TT
DFF = 512
NE = 16
NPASS = 4
EPP = NE // NPASS
GAMMA = 0.99
H, DK, DV = 16, 32, 32
HPC = 8
L = 128
NCH = T // L
GL = float(GAMMA ** L)
EPS = 1e-5
KD = D // 128
KF = DFF // 128
WIMG = KD * DFF
WIMGD = KF * D


def host_consts():
    ii = np.arange(L, dtype=np.float64)
    mg = np.where(ii[:, None] <= ii[None, :], GAMMA ** (ii[None, :] - ii[:, None]), 0.0)
    m5 = np.where(ii[:, None] < ii[None, :], GAMMA ** (ii[None, :] - ii[:, None]), 0.0)
    maskpack = np.concatenate([m5, mg], axis=1)
    gq5 = np.broadcast_to(GAMMA ** (ii + 1), (128, L)).copy()
    gcols = np.zeros((128, 4))
    gcols[:, 0] = GAMMA ** (ii + 1)
    gcols[:, 1] = -(GAMMA ** (ii + 1))
    gcols[:, 2] = GAMMA ** (L - 1 - ii)
    gcols[:, 3] = EPS
    sel = np.zeros((16, 16 * 128))
    for e in range(16):
        sel[e, e * 128:(e + 1) * 128] = 1.0
    return {
        "c_id128": np.eye(128, dtype=np.float32),
        "c_ones_k": np.ones((128, 1), np.float32),
        "c_ones_r": np.ones((1, 128), np.float32),
        "c_sel": sel.astype(np.float32),
        "c_maskpack": maskpack.astype(np.float32),
        "c_gq5": gq5.astype(np.float32),
        "c_gcols": gcols.astype(np.float32),
    }


CONST_SHAPES = {
    "c_id128": [128, 128], "c_ones_k": [128, 1], "c_ones_r": [1, 128],
    "c_sel": [16, 2048], "c_maskpack": [128, 256], "c_gq5": [128, 128],
    "c_gcols": [128, 4],
}


DEBUG = False

_UNIQ = [0]


def uniq(name):
    _UNIQ[0] += 1
    return f"{name}_{_UNIQ[0]}"


class Sem:
    def __init__(self, nc, name):
        self.h = nc.alloc_semaphore(uniq(name))
        self.n = 0


class Prog:
    ENGS = ("sync", "gpsimd", "scalar", "vector", "tensor")

    def __init__(self, nc):
        self.nc = nc
        self.th = {e: [] for e in self.ENGS}
        self._nseq = 0
        self.sq = Sem(nc, "sq0")
        self.prev_chain = None
        self._qs = {}
        self._pending = []

    def qsem(self, eng):
        if eng not in self._qs:
            self._qs[eng] = Sem(self.nc, f"q_{eng}")
        return self._qs[eng]

    def new_chain(self):
        self.prev_chain = (self.sq, self.sq.n)
        self._nseq += 1
        self.sq = Sem(self.nc, f"sq{self._nseq}")

    def step(self, eng, fn, dma=False, extra_waits=()):
        ew = list(extra_waits) + self._pending
        self._pending = []
        tgt, sem = self.sq.n, self.sq
        if dma:
            qs = self.qsem(eng)
            qs.n += 16
            qtgt = qs.n

            def th(e, _fn=fn, _t=tgt, _s=sem, _qs=qs, _ew=tuple(ew)):
                for (s2, v2) in _ew:
                    if v2 > 0:
                        e.wait_ge(s2.h, v2)
                if _t > 0:
                    e.wait_ge(_s.h, _t)
                _fn(e).then_inc(_qs.h, 16)
            self.th[eng].append(th)
            self._pending.append((qs, qtgt))
        else:
            def th(e, _fn=fn, _t=tgt, _s=sem, _ew=tuple(ew)):
                for (s2, v2) in _ew:
                    if v2 > 0:
                        e.wait_ge(s2.h, v2)
                if _t > 0:
                    e.wait_ge(_s.h, _t)
                _fn(e).then_inc(_s.h, 1)
            self.th[eng].append(th)
            self.sq.n += 1

    def raw(self, eng, fn):
        self.th[eng].append(fn)

    def issue(self, eng, fn, extra_waits=(), chain_wait=True):
        tgt, sem, ew = self.sq.n, self.sq, tuple(extra_waits)

        def th(e, _fn=fn, _t=tgt if chain_wait else 0, _s=sem, _ew=ew):
            for (s2, v2) in _ew:
                if v2 > 0:
                    e.wait_ge(s2.h, v2)
            if _t > 0:
                e.wait_ge(_s.h, _t)
            _fn(e)
        self.th[eng].append(th)

    def emit(self, name="blk"):
        nc, th = self.nc, self.th
        with nc.Block(name) as block:
            @block.sync
            def _(e):
                for f in th["sync"]:
                    f(e)

            @block.gpsimd
            def _(e):
                for f in th["gpsimd"]:
                    f(e)

            @block.scalar
            def _(e):
                for f in th["scalar"]:
                    f(e)

            @block.vector
            def _(e):
                for f in th["vector"]:
                    f(e)

            @block.tensor
            def _(e):
                for f in th["tensor"]:
                    f(e)
        self.th = {e: [] for e in self.ENGS}


class Ctx:
    pass


def make_nc():
    return bass.Bass("TRN2", target_bir_lowering=False, num_devices=NCORE)


def d3(x2d, p=128):
    return x2d.rearrange("(a p) n -> a p n", p=p).transpose([1, 0, 2])


def s3(sb2d, a):
    return sb2d.rearrange("p (a n) -> p a n", a=a)


def w_ap(w):
    return d3(w)


def w3_ap(w, e):
    return d3(w[e])


def decl_common(nc, c):
    c.wq = nc.dram_tensor("wq", [D, DFF], F32, kind="ExternalInput").ap()
    c.wk = nc.dram_tensor("wk", [D, DFF], F32, kind="ExternalInput").ap()
    c.wv = nc.dram_tensor("wv", [D, DFF], F32, kind="ExternalInput").ap()
    c.wo = nc.dram_tensor("wo", [DFF, D], F32, kind="ExternalInput").ap()
    c.gate_wT = nc.dram_tensor("gate_wT", [D, NE], F32, kind="ExternalInput").ap()
    c.we_gate = nc.dram_tensor("we_gate", [NE, D, DFF], F32, kind="ExternalInput").ap()
    c.we_up = nc.dram_tensor("we_up", [NE, D, DFF], F32, kind="ExternalInput").ap()
    c.we_down = nc.dram_tensor("we_down", [NE, DFF, D], F32, kind="ExternalInput").ap()
    c.ws_gate = nc.dram_tensor("ws_gate", [D, DFF], F32, kind="ExternalInput").ap()
    c.ws_up = nc.dram_tensor("ws_up", [D, DFF], F32, kind="ExternalInput").ap()
    c.ws_down = nc.dram_tensor("ws_down", [DFF, D], F32, kind="ExternalInput").ap()
    for n, shp in CONST_SHAPES.items():
        setattr(c, n, nc.dram_tensor(n, shp, F32, kind="ExternalInput").ap())
    c.id_f = nc.alloc_sbuf_tensor("id_f", [128, 128], F32).ap()
    c.id_b = nc.alloc_sbuf_tensor("id_b", [128, 128], BF16).ap()
    c.ones_k = nc.alloc_sbuf_tensor("ones_k", [128, 1], BF16).ap()
    c.ones_r = nc.alloc_sbuf_tensor("ones_r", [1, 128], F32).ap()
    c.sel_b = nc.alloc_sbuf_tensor("sel_b", [16, 2048], BF16).ap()
    c.maskp = nc.alloc_sbuf_tensor("maskp", [128, 256], F32).ap()
    c.gq5 = nc.alloc_sbuf_tensor("gq5", [128, 128], F32).ap()
    c.gcols = nc.alloc_sbuf_tensor("gcols", [128, 4], F32).ap()
    c.gatewT = nc.alloc_sbuf_tensor("gatewT", [128, KD * NE], F32).ap()


def load_consts(p, c, sC):
    def f(e):
        e.dma_start(c.id_f, c.c_id128).then_inc(sC.h, 16)
        e.dma_start(c.id_b, c.c_id128).then_inc(sC.h, 16)
        e.dma_start(c.ones_k, c.c_ones_k).then_inc(sC.h, 16)
        e.dma_start(c.ones_r, c.c_ones_r).then_inc(sC.h, 16)
        e.dma_start(c.sel_b, c.c_sel).then_inc(sC.h, 16)
        e.dma_start(c.maskp, c.c_maskpack).then_inc(sC.h, 16)
        e.dma_start(c.gq5, c.c_gq5).then_inc(sC.h, 16)
        e.dma_start(c.gcols, c.c_gcols).then_inc(sC.h, 16)
        e.dma_start(s3(c.gatewT, KD), d3(c.gate_wT)).then_inc(sC.h, 16)
    p.th["gpsimd"].append(f)
    sC.n += 9 * 16


class FfnBufs:
    def __init__(self, nc, stk):
        A = lambda n, s, d: stk.enter_context(nc.sbuf_tensor(uniq(n), s, d)).ap()
        self.wg = [A(f"wg{i}", [128, WIMG], BF16) for i in range(5)]
        self.wu = [A(f"wu{i}", [128, WIMG], BF16) for i in range(5)]
        self.wd = [A(f"wd{i}", [128, WIMGD], BF16) for i in range(5)]
        self.x_in = [A("x_in0", [128, 2048], F32), A("x_in1", [128, 2048], F32)]
        self.x_fm = A("x_fm", [128, 2048], F32)
        self.ln_f = A("ln_f", [128, 2048], F32)
        self.ln_b = [A("ln_b0", [128, 2048], BF16), A("ln_b1", [128, 2048], BF16)]
        self.rbs = [A("rbs0", [128, 1280], BF16), A("rbs1", [128, 1280], BF16)]
        self.scr = A("scr", [128, 4096], BF16)
        self.hh = [A("hh0", [128, 1024], BF16), A("hh1", [128, 1024], BF16)]
        self.xout = [A(f"xout{i}", [128, 2048], F32) for i in range(2)]
        self.stat = A("stat", [1, 1024], F32)
        self.scores = A("scores", [128, 32], F32)
        self.maxes = A("maxes", [128, 16], F32)
        self.rsm = A("rsm", [128, 16], F32)
        self.istmp = A("istmp", [128, 32], F32)
        self.route = A("route", [128, 32], F32)
        self.rt4 = A("rt4", [16, 1024], BF16)
        self.x_bf = self.scr[:, 0:2048]
        self.sq_bf = self.scr[:, 2048:4096]
        self.t1 = self.scr[:, 0:1024]
        self.t2 = self.scr[:, 1024:2048]
        self.o_in = self.wg[4][:, 0:1024]


class FfnPsum:
    def __init__(self, nc, stk):
        E = stk.enter_context
        self.g = E(nc.psum_tensor(uniq("pg"), [128, KF * TT], F32)).ap()
        self.u = E(nc.psum_tensor(uniq("pu"), [128, KF * TT], F32)).ap()
        self.yacc = E(nc.psum_tensor(uniq("pyacc"), [128, KD * TT], F32)).ap()
        self.misc = self.g[:, 0:512]
        self.rbst = self.u[:, 0:512]


def emit_ln(p, c, fb, pp, src, lnb):
    p.step("vector", lambda e: e.tensor_copy(fb.x_bf, src[:, 0:2048]))
    p.step("scalar", lambda e: e.activation(fb.sq_bf, src[:, 0:2048], AF.Square))

    def f(e):
        for k in range(KD):
            e.matmul(pp.misc[0:1, 0:256], c.ones_k,
                     fb.x_bf[:, k * 256:(k + 1) * 256],
                     start=(k == 0), stop=(k == KD - 1))
        for k in range(KD):
            last = e.matmul(pp.misc[0:1, 256:512], c.ones_k,
                            fb.sq_bf[:, k * 256:(k + 1) * 256],
                            start=(k == 0), stop=(k == KD - 1))
        return last
    p.step("tensor", f)

    def f(e):
        e.tensor_scalar_mul(fb.stat[:, 0:256], pp.misc[0:1, 0:256], 1.0 / D)
        e.tensor_scalar_mul(fb.stat[:, 256:512], pp.misc[0:1, 256:512], 1.0 / D)
        e.drain()
        e.tensor_mul(fb.stat[:, 512:768], fb.stat[:, 0:256], fb.stat[:, 0:256])
        e.drain()
        return e.tensor_sub(fb.stat[:, 512:768], fb.stat[:, 256:512],
                            fb.stat[:, 512:768])
    p.step("vector", f)
    p.step("scalar", lambda e: e.activation(fb.stat[:, 256:512], fb.stat[:, 512:768],
                                            AF.Sqrt, bias=c.gcols[0:1, 3:4]))

    def f(e):
        e.reciprocal(fb.stat[:, 512:768], fb.stat[:, 256:512])
        e.drain()
        return e.scalar_tensor_tensor(fb.stat[:, 768:1024], fb.stat[:, 0:256], -1.0,
                                      fb.stat[:, 512:768], ALU.mult, ALU.mult)
    p.step("vector", f)

    def f(e):
        e.matmul(pp.misc[:, 0:256], c.ones_r, fb.stat[:, 512:768],
                 start=True, stop=True)
        return e.matmul(pp.misc[:, 256:512], c.ones_r, fb.stat[:, 768:1024],
                        start=True, stop=True)
    p.step("tensor", f)

    def f(e):
        for k in range(KD):
            sl = slice(k * 256, (k + 1) * 256)
            e.tensor_mul(fb.ln_f[:, sl], src[:, sl], pp.misc[:, 0:256])
            e.drain()
            e.tensor_add(fb.ln_f[:, sl], fb.ln_f[:, sl], pp.misc[:, 256:512])
        e.drain()
        return e.tensor_copy(lnb[:], fb.ln_f[:])
    p.step("vector", f)


def emit_route(p, c, fb, pp):
    def f(e):
        for m in range(2):
            for k in range(KD):
                last = e.matmul(pp.misc[:, m * 16:(m + 1) * 16],
                                fb.ln_f[:, k * 256 + m * 128: k * 256 + m * 128 + 128],
                                c.gatewT[:, k * 16:(k + 1) * 16],
                                start=(k == 0), stop=(k == KD - 1))
        return last
    p.step("tensor", f)
    p.step("scalar", lambda e: e.activation(fb.scores[:], pp.misc[:, 0:32], AF.Sigmoid))

    def f(e):
        for m in range(2):
            sc = fb.scores[:, m * 16:(m + 1) * 16]
            mx = fb.maxes[:, m * 8:(m + 1) * 8]
            e.max(out=mx, in_=sc)
            e.drain()
            r = fb.rsm[:, m * 8:(m + 1) * 8]
            m1, m2 = mx[:, 0:1], mx[:, 1:2]
            den, rec, w1, w2 = r[:, 0:1], r[:, 1:2], r[:, 2:3], r[:, 3:4]
            e.tensor_scalar(den, m1, m2, 1e-20, ALU.add, ALU.add)
            e.drain()
            e.reciprocal(rec, den)
            e.drain()
            e.tensor_scalar(w1, m1, rec, 1.25, ALU.mult, ALU.mult)
            e.tensor_scalar(w2, m2, rec, 1.25, ALU.mult, ALU.mult)
            e.drain()
            rt = fb.route[:, m * 16:(m + 1) * 16]
            it = fb.istmp[:, m * 16:(m + 1) * 16]
            e.tensor_single_scalar(it, sc, m1, ALU.is_equal)
            e.drain()
            e.tensor_scalar_mul(rt, it, w1)
            e.drain()
            e.tensor_single_scalar(it, sc, m2, ALU.is_equal)
            e.drain()
            last = e.scalar_tensor_tensor(rt, it, w2, rt, ALU.mult, ALU.add)
            e.drain()
        return last
    p.step("vector", f)

    def f(e):
        e.transpose(pp.misc[0:16, 0:128], fb.route[:, 0:16], c.id_f)
        return e.transpose(pp.misc[0:16, 128:256], fb.route[:, 16:32], c.id_f)
    p.step("tensor", f)

    def f(e):
        for r in range(4):
            last = e.tensor_copy(fb.rt4[:, r * 256:(r + 1) * 256],
                                 pp.misc[0:16, 0:256])
        return last
    p.step("vector", f)


def emit_rb_all(p, c, fb, pp):
    for eid in range(NE):
        def fmm(e, _e=eid):
            return e.matmul(pp.rbst[:, 0:256] if _e % 2 == 0 else pp.rbst[:, 256:512],
                            c.sel_b[:, _e * 128:(_e + 1) * 128],
                            fb.rt4[:, 0:256], start=True, stop=True)
        p.step("tensor", fmm)

        def fcp(e, _e=eid):
            return e.copy(fb.scr[:, _e * 256:(_e + 1) * 256],
                          pp.rbst[:, 0:256] if _e % 2 == 0 else pp.rbst[:, 256:512])
        p.step("scalar", fcp)


class ESems:
    def __init__(self, nc, tag):
        self.gu = Sem(nc, f"eGU_{tag}")
        self.si = Sem(nc, f"eSI_{tag}")
        self.h = Sem(nc, f"eH_{tag}")
        self.dn = Sem(nc, f"eDN_{tag}")


def rb_bcast(rbs_par, sl):
    return rbs_par[:, sl * 256:(sl + 1) * 256] \
        .rearrange("p (o n) -> p o n", o=1).broadcast_to([128, KF, 256])


def emit_experts(p, c, fb, pp, slots, es, lnb, first_waits=(), dn_wait=None,
                 si_first_waits=(), h_first_waits=()):
    nslots = len(slots)
    chain0 = None
    pend = list(p._pending) + list(first_waits)
    p._pending = []

    def dn_of(i, wd, first, last_):
        def fd(e, _wd=wd, _first=first, _last=last_, _q=i % 2):
            for m in range(KD):
                for k in range(KF):
                    last = e.matmul(pp.yacc[:, m * TT:(m + 1) * TT],
                                    _wd[:, k * D + m * 128: k * D + m * 128 + 128],
                                    fb.hh[_q][:, k * TT:(k + 1) * TT],
                                    start=(_first and k == 0 and m % 2 == 0),
                                    stop=(_last and k == KF - 1),
                                    skip_group_check=True)
            return last
        return fd

    pend_dn = None
    for i, (wg, wu, wd, rb) in enumerate(slots):
        w_si, w_h = es.si.n, es.h.n
        cw = chain0 if i == 0 else None

        def fgu(e, _wg=wg, _wu=wu, _wsi=w_si, _wh=w_h, _cw=cw,
                _pend=tuple(pend) if i == 0 else ()):
            for (s2, v2) in _pend:
                if v2 > 0:
                    e.wait_ge(s2.h, v2)
            if _cw is not None and _cw[1] > 0:
                e.wait_ge(_cw[0].h, _cw[1])
            if _wsi > 0:
                e.wait_ge(es.si.h, _wsi)
            if _wh > 0:
                e.wait_ge(es.h.h, _wh)
            for m in range(KF):
                for k in range(KD):
                    e.matmul(pp.g[:, m * TT:(m + 1) * TT],
                             _wg[:, k * DFF + m * 128: k * DFF + m * 128 + 128],
                             lnb[:, k * TT:(k + 1) * TT],
                             start=(k == 0), stop=(k == KD - 1))
            for m in range(KF):
                for k in range(KD):
                    last = e.matmul(pp.u[:, m * TT:(m + 1) * TT],
                                    _wu[:, k * DFF + m * 128: k * DFF + m * 128 + 128],
                                    lnb[:, k * TT:(k + 1) * TT],
                                    start=(k == 0), stop=(k == KD - 1))
            last.then_inc(es.gu.h, 1)
            return last
        p.raw("tensor", fgu)
        es.gu.n += 1

        w_gu, w_h2 = es.gu.n, es.h.n

        def fsi(e, _wgu=w_gu, _wh=w_h2,
                _fw=tuple(si_first_waits) if i == 0 else ()):
            for (s2, v2) in _fw:
                if v2 > 0:
                    e.wait_ge(s2.h, v2)
            e.wait_ge(es.gu.h, _wgu)
            if _wh > 0:
                e.wait_ge(es.h.h, _wh)
            return e.activation(fb.t1, pp.g[:], AF.Silu).then_inc(es.si.h, 1)
        p.raw("scalar", fsi)
        es.si.n += 1

        w_si2, w_dn = es.si.n, es.dn.n

        def fh(e, _rb=rb, _wsi=w_si2, _wdn=w_dn, _q=i % 2,
               _fw=tuple(h_first_waits) if i == 0 else ()):
            for (s2, v2) in _fw:
                if v2 > 0:
                    e.wait_ge(s2.h, v2)
            e.wait_ge(es.si.h, _wsi)
            if _wdn > 0:
                e.wait_ge(es.dn.h, _wdn)
            e.tensor_mul(fb.t2, fb.t1, pp.u[:])
            e.drain()
            if _rb is None:
                last = e.tensor_scalar_mul(fb.hh[_q][:], fb.t2, 0.5)
            else:
                last = e.tensor_mul(s3(fb.hh[_q], KF), s3(fb.t2, KF), _rb)
            last.then_inc(es.h.h, 1)
            return last
        p.raw("vector", fh)
        es.h.n += 1

        if pend_dn is not None:
            fd_prev, w_h_prev, dnw = pend_dn

            def fdp(e, _fd=fd_prev, _wh=w_h_prev, _dnw=dnw):
                if _dnw is not None and _dnw[1] > 0:
                    e.wait_ge(_dnw[0].h, _dnw[1])
                e.wait_ge(es.h.h, _wh)
                _fd(e).then_inc(es.dn.h, 1)
            p.raw("tensor", fdp)
            es.dn.n += 1
        pend_dn = (dn_of(i, wd, i == 0, i == nslots - 1), es.h.n,
                   dn_wait if i == 0 else None)

    fd_last, w_h_last, dnw = pend_dn

    def fdl(e, _fd=fd_last, _wh=w_h_last, _dnw=dnw):
        if _dnw is not None and _dnw[1] > 0:
            e.wait_ge(_dnw[0].h, _dnw[1])
        e.wait_ge(es.h.h, _wh)
        _fd(e).then_inc(es.dn.h, 1)
    p.raw("tensor", fdl)
    es.dn.n += 1


def emit_ffn_phase(p, c, fb, io, phase):
    nc = p.nc
    sW = Sem(nc, f"sW_{phase}")
    sOUTg = Sem(nc, f"sOUTg_{phase}")
    sOUTs = Sem(nc, f"sOUTs_{phase}")
    sINs = [Sem(nc, f"sINs{i}_{phase}") for i in range(2)]
    sINg = [Sem(nc, f"sINg{i}_{phase}") for i in range(2)]
    stk_ph = contextlib.ExitStack()
    pp = FfnPsum(nc, stk_ph)
    es = ESems(nc, phase)

    p.new_chain()
    pcL = p.prev_chain
    if phase == "p2":
        def fwo(e, _pc=pcL):
            if _pc is not None and _pc[1] > 0:
                e.wait_ge(_pc[0].h, _pc[1])
            e.dma_start(s3(fb.wd[4], KF), w_ap(c.wo)).then_inc(sW.h, 16)
        p.th["gpsimd"].append(fwo)
        sW.n += 16
    wo_tgt = sW.n

    if io.cast_weights:
        sWc = [Sem(nc, f"sWc{j}_p0") for j in range(3)]
        sWcw = [Sem(nc, f"sWcw{j}_p0") for j in range(3)]
        srcs3 = [lambda e_: w3_ap(c.we_gate, e_), lambda e_: w3_ap(c.we_up, e_),
                 lambda e_: w3_ap(c.we_down, e_)]
        stg = [fb.wg[4], fb.wu[4], fb.wd[4]]
        kds = [KD, KD, KF]

        def fcast_w(e, _pc=pcL):
            if _pc is not None and _pc[1] > 0:
                e.wait_ge(_pc[0].h, _pc[1])
            for eid in range(NE + 1):
                for j in range(3):
                    if eid >= 1:
                        e.wait_ge(sWcw[j].h, eid * 16)
                    if eid <= NE - 1:
                        e.dma_start(s3(stg[j], kds[j]), srcs3[j](eid)
                                    ).then_inc(sWc[j].h, 16)
                    else:
                        wsrc = [c.ws_gate, c.ws_up, c.ws_down][j]
                        e.dma_start(s3(stg[j], kds[j]), w_ap(wsrc)
                                    ).then_inc(sWc[j].h, 16)
        p.th["gpsimd"].append(fcast_w)

        def fwrite_w(e):
            for eid in range(NE + 1):
                for j in range(3):
                    e.wait_ge(sWc[j].h, (eid + 1) * 16)
                    e.dma_start(io.wcache_w[j][eid], stg[j][:]
                                ).then_inc(sWcw[j].h, 16)
        p.th["sync"].append(fwrite_w)
        for j in range(3):
            sWc[j].n = 16 * (NE + 1)
            sWcw[j].n = 16 * (NE + 1)
        cache_ready = 16 * (NE + 1)
    else:
        sWcw, cache_ready = None, 0

    for t in range(NT):
        ew = [(sOUTg, sOUTg.n), (sOUTs, sOUTs.n)]
        if t == 0:
            ew.append((io.sC, io.sC.n))
            ew.append((sW, wo_tgt))
        ewx = ew
        pc_extra = pcL if t == 0 else None

        if phase == "p0":
            def fx(e, _t=t, _pc=pc_extra, _ew=tuple(ewx)):
                for (s2, v2) in _ew:
                    if v2 > 0:
                        e.wait_ge(s2.h, v2)
                if _pc is not None and _pc[1] > 0:
                    e.wait_ge(_pc[0].h, _pc[1])
                return e.dma_start(s3(fb.x_in[0], 2),
                                   d3(io.x[_t * TT:(_t + 1) * TT, :]))
            p.step("sync", fx, dma=True)

            def ftr(e):
                for s in range(2):
                    for fc in range(KD):
                        last = e.transpose(
                            pp.yacc[:, (fc * 2 + s) * 128:(fc * 2 + s) * 128 + 128],
                            fb.x_in[0][:, s * 1024 + fc * 128: s * 1024 + fc * 128 + 128],
                            c.id_f)
                return last
            p.step("tensor", ftr)
            p.step("vector", lambda e: e.tensor_copy(fb.x_fm[:], pp.yacc[:]))
        else:
            def fx(e, _t=t, _pc=pc_extra, _ew=tuple(ewx)):
                for (s2, v2) in _ew:
                    if v2 > 0:
                        e.wait_ge(s2.h, v2)
                if _pc is not None and _pc[1] > 0:
                    e.wait_ge(_pc[0].h, _pc[1])
                return e.dma_start(fb.x_in[0][:],
                                   io.x1[:, _t * 2048:(_t + 1) * 2048])
            p.step("sync", fx, dma=True)

            def fo(e, _t=t):
                return io.o_load(e, _t)
            p.step("gpsimd", fo, dma=True)

            def fatt(e):
                for m in range(KD):
                    for k in range(KF):
                        last = e.matmul(
                            pp.yacc[:, m * TT:(m + 1) * TT],
                            fb.wd[4][:, k * D + m * 128: k * D + m * 128 + 128],
                            fb.o_in[:, k * TT:(k + 1) * TT],
                            start=(k == 0), stop=(k == KF - 1))
                return last
            p.step("tensor", fatt)
            p.step("vector", lambda e: e.tensor_add(fb.x_fm[:], fb.x_in[0][:],
                                                    pp.yacc[:]))

        def fwx(e, _t=t):
            e.dma_start(io.x1[:, _t * 2048:(_t + 1) * 2048],
                        fb.x_fm[:]).then_inc(sOUTs.h, 16)
        p.issue("sync", fwx)
        sOUTs.n += 16

        emit_ln(p, c, fb, pp, fb.x_fm, fb.ln_b[0])
        emit_route(p, c, fb, pp)

        def fsc(e, _t=t):
            e.dma_start(io.lnsc[:, _t * 2048:(_t + 1) * 2048],
                        fb.ln_b[0][:]).then_inc(sOUTg.h, 16)
        p.issue("gpsimd", fsc)
        sOUTg.n += 16

        emit_rb_all(p, c, fb, pp)

        def frb(e, _t=t):
            e.dma_start(io.rbsc[_t], s3(fb.scr[:, 0:NE * 256], NE)).then_inc(sOUTg.h, 16)
        p.issue("gpsimd", frb)
        sOUTg.n += 16

    for pi in range(NPASS):
        p.new_chain()
        pc = p.prev_chain
        el = list(range(pi * EPP, (pi + 1) * EPP))
        with_shared = (pi == 0) if phase == "p0" else (pi == NPASS - 1)
        last_pass = (pi == NPASS - 1)

        wc = io.wcache if io.wcache is not None else io.wcache_w

        def fwl(e, _el=tuple(el), _ws=with_shared, _pi=pi, _pc=pc,
                _sg=sOUTg.n, _ss=sOUTs.n, _wc=wc,
                _cr=cache_ready if (io.cast_weights and pi == 0) else 0):
            if _pc is not None and _pc[1] > 0:
                e.wait_ge(_pc[0].h, _pc[1])
            if _cr > 0:
                for _s in sWcw:
                    e.wait_ge(_s.h, _cr)
            if _sg > 0:
                e.wait_ge(sOUTg.h, _sg)
            if _ss > 0:
                e.wait_ge(sOUTs.h, _ss)
            for sl, eid in enumerate(_el):
                e.dma_start(fb.wg[sl][:], _wc[0][eid]).then_inc(sW.h, 16)
                e.dma_start(fb.wu[sl][:], _wc[1][eid]).then_inc(sW.h, 16)
                e.dma_start(fb.wd[sl][:], _wc[2][eid]).then_inc(sW.h, 16)
            if _ws:
                e.dma_start(fb.wg[4][:], _wc[0][NE]).then_inc(sW.h, 16)
                e.dma_start(fb.wu[4][:], _wc[1][NE]).then_inc(sW.h, 16)
                e.dma_start(fb.wd[4][:], _wc[2][NE]).then_inc(sW.h, 16)
            if phase == "p0" and _pi == NPASS - 1:
                e.dma_start(s3(fb.wg[4], KD), w_ap(c.wq)).then_inc(sW.h, 16)
                e.dma_start(s3(fb.wu[4], KD), w_ap(c.wk)).then_inc(sW.h, 16)
                e.dma_start(s3(fb.wd[4], KD), w_ap(c.wv)).then_inc(sW.h, 16)
        p.th["gpsimd"].append(fwl)
        nw = 3 * len(el) + (3 if with_shared else 0)
        if phase == "p0" and pi == NPASS - 1:
            nw += 3
        sW.n += 16 * nw
        wload_tgt = sW.n

        in_tgt = []

        def load_tile(t, e0=el[0], _pc0=pc):
            par = t % 2
            pcw = _pc0 if t == 0 else None
            outw = (sOUTg.n, sOUTs.n) if t == 0 else None

            def fls(e, _t=t, _par=par, _pc=pcw, _ow=outw):
                if _pc is not None and _pc[1] > 0:
                    e.wait_ge(_pc[0].h, _pc[1])
                if _ow is not None:
                    if _ow[0] > 0:
                        e.wait_ge(sOUTg.h, _ow[0])
                    if _ow[1] > 0:
                        e.wait_ge(sOUTs.h, _ow[1])
                e.dma_start(fb.x_in[_par][:],
                            io.x1[:, _t * 2048:(_t + 1) * 2048]
                            ).then_inc(sINs[_par].h, 16)
            p.issue("sync", fls)
            sINs[par].n += 16

            def flg(e, _t=t, _par=par, _e0=e0, _pc=pcw, _ow=outw):
                if _pc is not None and _pc[1] > 0:
                    e.wait_ge(_pc[0].h, _pc[1])
                if _ow is not None:
                    if _ow[0] > 0:
                        e.wait_ge(sOUTg.h, _ow[0])
                    if _ow[1] > 0:
                        e.wait_ge(sOUTs.h, _ow[1])
                e.dma_start(fb.ln_b[_par][:],
                            io.lnsc[:, _t * 2048:(_t + 1) * 2048]
                            ).then_inc(sINg[_par].h, 16)
                e.dma_start(s3(fb.rbs[_par][:, 0:EPP * 256], EPP),
                            io.rbsc_sl(_t, _e0)).then_inc(sINg[_par].h, 16)
            p.issue("scalar", flg)
            sINg[par].n += 32
            in_tgt.append((sINs[par].n, sINg[par].n))

        load_tile(0)
        for t in range(NT):
            par = t % 2
            if t + 1 < NT:
                load_tile(t + 1)
            ws, wg_ = in_tgt[t]
            ew = [(sINg[par], wg_)]
            if t == 0:
                ew.append((sW, wload_tgt))
            if last_pass and phase == "p0":
                ew.append((p.sq, p.sq.n))
            dn_w = (p.sq, p.sq.n)
            si_w = [(sOUTg, sOUTg.n)]
            h_w = [(sOUTg, sOUTg.n)]

            slots = [(fb.wg[sl], fb.wu[sl], fb.wd[sl], rb_bcast(fb.rbs[par], sl))
                     for sl in range(EPP)]
            if with_shared:
                slots.append((fb.wg[4], fb.wu[4], fb.wd[4], None))
            emit_experts(p, c, fb, pp, slots, es, fb.ln_b[par],
                         first_waits=ew, dn_wait=dn_w,
                         si_first_waits=si_w, h_first_waits=h_w)

            p.step("vector", lambda e, _par=par:
                   e.tensor_add(fb.xout[_par][:], fb.x_in[_par][:, 0:2048], pp.yacc[:]),
                   extra_waits=[(es.dn, es.dn.n), (sINs[par], ws),
                                (sOUTs, sOUTs.n)])

            if not (last_pass and phase == "p2"):
                def fw(e, _t=t, _par=par):
                    e.dma_start(io.x1[:, _t * 2048:(_t + 1) * 2048],
                                fb.xout[_par][:]).then_inc(sOUTs.h, 16)
                p.issue("sync", fw)
                sOUTs.n += 16

            if last_pass and phase == "p0":
                emit_ln(p, c, fb, pp, fb.xout[par], fb.ln_b[par])

                def fq(e, _par=par):
                    dsts = [pp.g, pp.u, pp.yacc]
                    for j in range(3):
                        srcw = [fb.wg[4], fb.wu[4], fb.wd[4]][j]
                        for m in range(KF):
                            for k in range(KD):
                                last = e.matmul(
                                    dsts[j][:, m * TT:(m + 1) * TT],
                                    srcw[:, k * DFF + m * 128: k * DFF + m * 128 + 128],
                                    fb.ln_b[_par][:, k * TT:(k + 1) * TT],
                                    start=(k == 0), stop=(k == KD - 1))
                    return last
                p.step("tensor", fq)

                def fqe(e):
                    e.tensor_copy(fb.t1, pp.g[:])
                    e.tensor_copy(fb.t2, pp.u[:])
                    return e.tensor_copy(fb.hh[0][:], pp.yacc[:, 0:1024])
                p.step("vector", fqe)

                def fqw(e, _t=t):
                    io.qkv_out(e, _t, sOUTg)
                p.issue("gpsimd", fqw)
                sOUTg.n += 16 * io.qkv_ndma

            if last_pass and phase == "p2":
                def ftr2(e, _par=par):
                    for s in range(2):
                        for fc in range(KD):
                            last = e.transpose(
                                pp.yacc[:, (s * KD + fc) * 128:(s * KD + fc) * 128 + 128],
                                fb.xout[_par][:, fc * 256 + s * 128: fc * 256 + s * 128 + 128],
                                c.id_f)
                    return last
                p.step("tensor", ftr2)
                p.step("vector", lambda e: e.tensor_copy(fb.x_fm[:], pp.yacc[:]))

                def fy(e, _t=t):
                    return e.dma_start(
                        d3(io.y[_t * TT:(_t + 1) * TT, :]),
                        s3(fb.x_fm, 2))
                p.step("sync", fy, dma=True)

    stk_ph.close()

    def fdr(e, _g=sOUTg.n, _s=sOUTs.n,
            _qg=p.qsem("gpsimd").n, _qs=p.qsem("sync").n):
        if _g:
            e.wait_ge(sOUTg.h, _g)
        if _s:
            e.wait_ge(sOUTs.h, _s)
        if _qg:
            e.wait_ge(p.qsem("gpsimd").h, _qg)
        if _qs:
            e.wait_ge(p.qsem("sync").h, _qs)
        return e.nop()
    p.step("gpsimd", fdr)
    return sOUTg


NSET = 3


class ScanBufs:
    def __init__(self, nc, stk):
        A = lambda n, s, d: stk.enter_context(nc.sbuf_tensor(uniq(n), s, d)).ap()
        self.qc = [A(f"qc{i}", [128, 768], BF16) for i in range(2)]
        self.qf = [A(f"qf{i}", [32, 3072], BF16) for i in range(2)]
        self.tok = A("tok", [128, 768], BF16)
        self.q5 = A("q5", [32, 1024], BF16)
        self.kw = A("kw", [128, 256], BF16)
        self.qw = A("qw", [128, 256], BF16)
        self.scg = A("scg", [32, 768], F32)
        self.scgb = A("scgb", [32, 768], BF16)
        self.aw = [A(f"awsc{i}", [128, 448], BF16) for i in range(NSET)]
        self.bm = [A(f"bm{i}", [128, 256], BF16) for i in range(NSET)]
        self.fm = [A(f"fmsc{i}", [128, 128], BF16) for i in range(NSET)]
        self.dm = [A(f"dmsc{i}", [128, 128], BF16) for i in range(NSET)]
        self.kcn = [A(f"kcn{i}", [128, 32], BF16) for i in range(NSET)]
        self.obt = [A(f"obt{i}", [128, 32], F32) for i in range(NSET)]
        self.osb = A("osb", [128, 256], BF16)
        self.ofm = A("ofm", [128, 256], BF16)
        self.gq5b = A("gq5b", [32, 128], F32)


class ScanPsum:
    def __init__(self, nc, stk):
        E = stk.enter_context
        self.aq = [E(nc.psum_tensor(uniq("saq"), [128, 512], F32)).ap()
                   for _ in range(NSET)]
        self.fdq = [E(nc.psum_tensor(uniq("sfdq"), [128, 512], F32)).ap()
                    for _ in range(NSET)]
        tb = E(nc.psum_tensor(uniq("stok"), [128, 1024], BF16)).ap()
        self.tokp = tb[:, 0:768]
        self.ofmp = tb[:, 768:1024]


class SSems:
    def __init__(self, nc):
        for n in ("t1", "s2", "v3", "t4", "v5", "t6", "v7", "t8", "v10"):
            setattr(self, n, Sem(nc, f"sc_{n}"))


def emit_scan(p, c, io):
    nc = p.nc
    p.new_chain()
    pc = p.prev_chain
    sX = [Sem(nc, "sX_scan0"), Sem(nc, "sX_scan1")]
    sOUT = Sem(nc, "sOUT_scan")
    ss = SSems(nc)
    with contextlib.ExitStack() as stk:
        sb = ScanBufs(nc, stk)
        ps = ScanPsum(nc, stk)

        def fz(e, _pc=pc):
            if _pc is not None and _pc[1] > 0:
                e.wait_ge(_pc[0].h, _pc[1])
            e.memset(sb.scg[:], 0.0)
            for s in range(NSET):
                e.memset(ps.aq[s][:], 0.0)
                e.memset(ps.fdq[s][:], 0.0)
            return e.memset(sb.gq5b[:], 0.0)
        p.step("vector", fz, extra_waits=[(io.sC, io.sC.n)])
        p.step("vector", lambda e: e.tensor_copy(sb.gq5b[:], c.gq5[0:32, :]))

        gpair = 0
        for ch in range(NCH):
            par = ch % 2

            def fqd(e, _ch=ch, _par=par, _t8=ss.t8.n):
                if _t8 > 0:
                    e.wait_ge(ss.t8.h, _t8)
                io.qkv_in(e, _ch, sb.qc[_par], sb.qf[_par], sX[_par])
            p.issue("gpsimd", fqd, chain_wait=False)
            sX[par].n += 32

            def fcast(e, _v10=ss.v10.n, _o=sOUT.n):
                e.wait_ge(ss.v10.h, _v10) if _v10 > 0 else None
                if _o > 0:
                    e.wait_ge(sOUT.h, _o)
                return e.tensor_copy(sb.scgb[:], sb.scg[:])
            p.step("vector", fcast, extra_waits=[(sX[par], sX[par].n)])

            qs, qf = sb.qc[par], sb.qf[par]

            def ftr(e, _qs=qs):
                for s in range(6):
                    last = e.transpose(ps.tokp[:, s * 128:(s + 1) * 128],
                                       _qs[:, s * 128:(s + 1) * 128], c.id_b)
                return last
            p.step("tensor", ftr)
            p.step("scalar", lambda e: e.copy(sb.tok[:], ps.tokp[:]))

            def fsc(e, _qf=qf):
                for pr8 in range(8):
                    e.tensor_mul(sb.q5[:, pr8 * 128:(pr8 + 1) * 128],
                                 _qf[:, pr8 * 128:(pr8 + 1) * 128], sb.gq5b[:])
                e.tensor_scalar_mul(sb.kw[:], sb.tok[:, 256:512], c.gcols[:, 2:3])
                return e.tensor_scalar_mul(sb.qw[:], sb.tok[:, 0:256], c.gcols[:, 2:3])
            p.step("vector", fsc)
            head = (p.sq, p.sq.n)

            stA, stB, stC, stD = [], [], [], []
            svS, svV = [], []
            for pr in range(8):
                g = gpair + pr
                st = g % NSET
                aq, fdq = ps.aq[st], ps.fdq[st]
                aw, bm, fm, dm, kcn, obt = (sb.aw[st], sb.bm[st], sb.fm[st],
                                            sb.dm[st], sb.kcn[st], sb.obt[st])
                s_, r = pr // 4, (pr % 4) * 32
                qff = qf[:, pr * 128:(pr + 1) * 128]
                kff = qf[:, (8 + pr) * 128:(9 + pr) * 128]
                q5f = sb.q5[:, pr * 128:(pr + 1) * 128]
                ktk = sb.tok[:, 256 + s_ * 128 + r: 256 + s_ * 128 + r + 32]
                vtk = sb.tok[:, 512 + s_ * 128 + r: 512 + s_ * 128 + r + 32]
                kwk = sb.kw[:, s_ * 128 + r: s_ * 128 + r + 32]
                qwk = sb.qw[:, s_ * 128 + r: s_ * 128 + r + 32]
                Ssl = sb.scgb[:, pr * 32:(pr + 1) * 32]
                Csl = sb.scgb[:, 256 + pr * 32: 256 + (pr + 1) * 32]
                Gsl = sb.scgb[:, 512 + pr * 32: 512 + (pr + 1) * 32]

                def f1(e, _q=qff, _k=kff, _S=Ssl, _C=Csl, _aq=aq, _g=g,
                       _head=head):
                    e.wait_ge(_head[0].h, _head[1])
                    if _g - (NSET - 1) > 0:
                        e.wait_ge(ss.v10.h, _g - (NSET - 1))
                    e.matmul(_aq[:, 0:128], _q, _k, start=True, stop=True)
                    e.matmul(_aq[:, 128:256], _k, _q, start=True, stop=True)
                    e.matmul(_aq[0:32, 256:384], _S, _q, start=True, stop=True)
                    e.matmul(_aq[0:32, 384:416], _S, _C, start=True, stop=True)
                    return e.matmul(_aq[:, 416:448], _k, _C,
                                    start=True, stop=True).then_inc(ss.t1.h, 1)
                stA.append(f1)

                def f2(e, _aq=aq, _aw=aw, _g=g):
                    e.wait_ge(ss.t1.h, _g + 1)
                    return e.copy(_aw[:], _aq[:, 0:448]).then_inc(ss.s2.h, 1)
                svS.append(f2)

                def f3(e, _aw=aw, _bm=bm, _g=g):
                    e.wait_ge(ss.s2.h, _g + 1)
                    return e.tensor_mul(_bm[:], _aw[:, 0:256],
                                        c.maskp[:]).then_inc(ss.v3.h, 1)
                v3 = f3

                def f4(e, _q=qff, _aw=aw, _fdq=fdq, _g=g):
                    e.wait_ge(ss.v3.h, _g + 1)
                    return e.matmul(_fdq[:, 0:128], _q, _aw[0:32, 256:384],
                                    start=True, stop=True).then_inc(ss.t4.h, 1)
                stB.append(f4)

                def f5(e, _fdq=fdq, _fm=fm, _g=g):
                    e.wait_ge(ss.t4.h, _g + 1)
                    return e.tensor_mul(_fm[:], _fdq[:, 0:128],
                                        c.maskp[:, 128:256]).then_inc(ss.v5.h, 1)
                v5 = f5

                def f6(e, _aw=aw, _bm=bm, _v=vtk, _aq=aq, _fdq=fdq, _g=g):
                    e.wait_ge(ss.v5.h, _g + 1)
                    e.matmul(_fdq[:, 128:256], _aw[:, 128:256], _bm[:, 128:256],
                             start=True, stop=True)
                    return e.matmul(_aq[:, 448:480], _bm[:, 0:128], _v,
                                    start=True, stop=True).then_inc(ss.t6.h, 1)
                stC.append(f6)

                def f7(e, _fdq=fdq, _dm=dm, _aw=aw, _kcn=kcn, _aq=aq, _g=g):
                    e.wait_ge(ss.t6.h, _g + 1)
                    e.tensor_mul(_dm[:], _fdq[:, 128:256], c.maskp[:, 128:256])
                    return e.scalar_tensor_tensor(
                        _kcn[:], _aw[:, 416:448], c.gcols[:, 1:2],
                        _aq[:, 448:480], ALU.mult, ALU.subtract
                    ).then_inc(ss.v7.h, 1)
                v7 = f7

                def f8(e, _q5=q5f, _q=qff, _G=Gsl, _v=vtk, _aw=aw, _bm=bm,
                       _fm=fm, _dm=dm, _kcn=kcn, _fdq=fdq, _kw=kwk, _qw=qwk,
                       _k=ktk, _g=g):
                    e.wait_ge(ss.v7.h, _g + 1)
                    e.matmul(_fdq[:, 256:288], _q5, _aw[0:32, 384:416],
                             start=True, stop=False)
                    e.matmul(_fdq[:, 256:288], _q, _G, start=False, stop=False)
                    e.matmul(_fdq[:, 256:288], _fm[:], _v, start=False, stop=False)
                    e.matmul(_fdq[:, 256:288], _bm[:, 128:256], _aw[:, 416:448],
                             start=False, stop=True)
                    e.matmul(_fdq[:, 288:320], _dm[:], _v, start=True, stop=False)
                    e.matmul(_fdq[:, 288:320], _bm[:, 128:256], _kcn[:],
                             start=False, stop=True)
                    e.matmul(_fdq[0:32, 320:352], _kw, _k, start=True, stop=True)
                    e.matmul(_fdq[0:32, 352:384], _qw, _v, start=True, stop=True)
                    return e.matmul(_fdq[0:32, 384:416], _kw, _kcn[:],
                                    start=True, stop=True).then_inc(ss.t8.h, 1)
                stD.append(f8)

                def f10(e, _fdq=fdq, _obt=obt, _pr=pr, _g=g):
                    e.wait_ge(ss.t8.h, _g + 1)
                    e.tensor_scalar_mul(_obt[:], _fdq[:, 256:288], c.gcols[:, 0:1])
                    e.drain()
                    e.tensor_add(sb.osb[:, _pr * 32:(_pr + 1) * 32],
                                 _obt[:], _fdq[:, 288:320])
                    Sf = sb.scg[:, _pr * 32:(_pr + 1) * 32]
                    Cf = sb.scg[:, 256 + _pr * 32: 256 + (_pr + 1) * 32]
                    Gf = sb.scg[:, 512 + _pr * 32: 512 + (_pr + 1) * 32]
                    e.scalar_tensor_tensor(Sf, Sf, GL, _fdq[0:32, 320:352],
                                           ALU.mult, ALU.add)
                    e.scalar_tensor_tensor(Cf, Cf, GL, _fdq[0:32, 352:384],
                                           ALU.mult, ALU.add)
                    return e.scalar_tensor_tensor(
                        Gf, Gf, GL, _fdq[0:32, 384:416],
                        ALU.mult, ALU.add).then_inc(ss.v10.h, 1)
                svV.append((v3, v5, v7, f10))

            ss.t1.n += 8
            ss.s2.n += 8
            ss.v3.n += 8
            ss.t4.n += 8
            ss.v5.n += 8
            ss.t6.n += 8
            ss.v7.n += 8
            ss.t8.n += 8
            ss.v10.n += 8
            gpair += 8

            for s in range(8 + 3):
                if 3 <= s and s - 3 < 8:
                    p.raw("tensor", stD[s - 3])
                    p.raw("vector", svV[s - 3][3])
                if 2 <= s and s - 2 < 8:
                    p.raw("tensor", stC[s - 2])
                    p.raw("vector", svV[s - 2][2])
                if 1 <= s and s - 1 < 8:
                    p.raw("tensor", stB[s - 1])
                    p.raw("vector", svV[s - 1][1])
                if s < 8:
                    p.raw("tensor", stA[s])
                    p.raw("scalar", svS[s])
                    p.raw("vector", svV[s][0])

            def fot(e, _v10=ss.v10.n):
                e.wait_ge(ss.v10.h, _v10)
                e.transpose(ps.ofmp[:, 0:128], sb.osb[:, 0:128], c.id_b)
                return e.transpose(ps.ofmp[:, 128:256], sb.osb[:, 128:256], c.id_b)
            p.step("tensor", fot)
            p.step("scalar", lambda e: e.copy(sb.ofm[:], ps.ofmp[:]))

            def fow(e, _ch=ch):
                io.o_out(e, _ch, sb.ofm, sOUT)
            p.issue("gpsimd", fow)
            sOUT.n += 16 * io.o_ndma

        def fdr(e, _n=sOUT.n):
            if _n:
                e.wait_ge(sOUT.h, _n)
            return e.nop()
        p.step("gpsimd", fdr)
    return sOUT


def build_p0_program():
    nc = make_nc()
    c = Ctx()
    decl_common(nc, c)
    io = Ctx()
    io.x = nc.dram_tensor("x", [TOKC, D], F32, kind="ExternalInput").ap()
    io.x1 = nc.dram_tensor("x1", [128, NT * 2048], F32, kind="ExternalOutput").ap()
    io.lnsc = nc.dram_tensor("lnsc", [128, NT * 2048], BF16, kind="Internal").ap()
    rb_scr = nc.dram_tensor("rb_scr", [NT, NE, 128, 256], BF16, kind="Internal").ap()
    io.rbsc = [rb_scr[t].transpose([1, 0, 2]) for t in range(NT)]
    io.rbsc_sl = lambda t, e0: rb_scr[t, e0:e0 + EPP].transpose([1, 0, 2])
    io.qkvd = nc.dram_tensor("qkv", [3, DFF, TOKC], BF16, kind="ExternalOutput").ap()
    wc_g = nc.dram_tensor("wc_g", [NE + 1, 128, WIMG], BF16, kind="ExternalOutput").ap()
    wc_u = nc.dram_tensor("wc_u", [NE + 1, 128, WIMG], BF16, kind="ExternalOutput").ap()
    wc_d = nc.dram_tensor("wc_d", [NE + 1, 128, WIMGD], BF16, kind="ExternalOutput").ap()
    io.wcache = None
    io.wcache_w = ([wc_g[i] for i in range(NE + 1)],
                   [wc_u[i] for i in range(NE + 1)],
                   [wc_d[i] for i in range(NE + 1)])
    io.cast_weights = True

    def qkv_out(e, t, sem, _io=io):
        for j, src in enumerate([io._fb.t1, io._fb.t2, io._fb.hh[0]]):
            e.dma_start(
                d3(_io.qkvd[j, :, t * TT:(t + 1) * TT]),
                s3(src, KF)).then_inc(sem.h, 16)
    io.qkv_out = qkv_out
    io.qkv_ndma = 3

    p = Prog(nc)
    sC = Sem(nc, "sC")
    io.sC = sC
    load_consts(p, c, sC)
    with contextlib.ExitStack() as stk:
        fb = FfnBufs(nc, stk)
        io._fb = fb
        emit_ffn_phase(p, c, fb, io, "p0")
        p.emit("p0")
    return nc


def build_scan_program():
    nc = make_nc()
    c = Ctx()
    decl_common(nc, c)
    io = Ctx()
    io.qkv_me = nc.dram_tensor("qkv_me", [NCH, 768, L], BF16, kind="ExternalInput").ap()
    io.o_me = nc.dram_tensor("o_me", [NCH, 2 * L, L], BF16, kind="ExternalOutput").ap()

    def qkv_in(e, ch, qc, qf, sem, _io=io):
        e.dma_start(s3(qc, 6), d3(_io.qkv_me[ch])).then_inc(sem.h, 16)
        e.dma_start(s3(qf, 24), d3(_io.qkv_me[ch], p=32)).then_inc(sem.h, 16)
    io.qkv_in = qkv_in

    def o_out(e, ch, ofm, sem, _io=io):
        e.dma_start(_io.o_me[ch, 0:128, :], ofm[:, 0:128]).then_inc(sem.h, 16)
        e.dma_start(_io.o_me[ch, 128:256, :], ofm[:, 128:256]).then_inc(sem.h, 16)
    io.o_out = o_out
    io.o_ndma = 2

    p = Prog(nc)
    sC = Sem(nc, "sC")
    io.sC = sC
    load_consts(p, c, sC)
    emit_scan(p, c, io)
    p.emit("scan")
    return nc


def build_p2_program():
    nc = make_nc()
    c = Ctx()
    decl_common(nc, c)
    io = Ctx()
    io.x1 = nc.dram_tensor("x1", [128, NT * 2048], F32, kind="ExternalInput").ap()
    io.o_mine = nc.dram_tensor("o_mine", [DFF, TOKC], BF16, kind="ExternalInput").ap()
    io.y = nc.dram_tensor("y", [TOKC, D], F32, kind="ExternalOutput").ap()
    io.lnsc = nc.dram_tensor("lnsc", [128, NT * 2048], BF16, kind="Internal").ap()
    rb_scr = nc.dram_tensor("rb_scr", [NT, NE, 128, 256], BF16, kind="Internal").ap()
    io.rbsc = [rb_scr[t].transpose([1, 0, 2]) for t in range(NT)]
    io.rbsc_sl = lambda t, e0: rb_scr[t, e0:e0 + EPP].transpose([1, 0, 2])
    wc_g = nc.dram_tensor("wc_g", [NE + 1, 128, WIMG], BF16, kind="ExternalInput").ap()
    wc_u = nc.dram_tensor("wc_u", [NE + 1, 128, WIMG], BF16, kind="ExternalInput").ap()
    wc_d = nc.dram_tensor("wc_d", [NE + 1, 128, WIMGD], BF16, kind="ExternalInput").ap()
    io.wcache = ([wc_g[i] for i in range(NE + 1)],
                 [wc_u[i] for i in range(NE + 1)],
                 [wc_d[i] for i in range(NE + 1)])
    io.cast_weights = False

    def o_load(e, t, _io=io):
        return e.dma_start(
            s3(_io._fb.o_in, KF),
            d3(_io.o_mine[:, t * TT:(t + 1) * TT]))
    io.o_load = o_load

    p = Prog(nc)
    sC = Sem(nc, "sC")
    io.sC = sC
    load_consts(p, c, sC)
    with contextlib.ExitStack() as stk:
        fb = FfnBufs(nc, stk)
        io._fb = fb
        emit_ffn_phase(p, c, fb, io, "p2")
        p.emit("p2")
    return nc


def kernel(L0_attn_x, L0_attn_Wq, L0_attn_Wk, L0_attn_Wv, L0_attn_Wo,
           L0_ffn_gate_w, L0_ffn_w_gate, L0_ffn_w_up, L0_ffn_w_down,
           L0_ffn_ws_gate, L0_ffn_ws_up, L0_ffn_ws_down):
    x = np.ascontiguousarray(np.asarray(L0_attn_x, np.float32))
    consts = host_consts()
    wmap = dict(
        wq=np.ascontiguousarray(L0_attn_Wq, np.float32),
        wk=np.ascontiguousarray(L0_attn_Wk, np.float32),
        wv=np.ascontiguousarray(L0_attn_Wv, np.float32),
        wo=np.ascontiguousarray(L0_attn_Wo, np.float32),
        gate_wT=np.ascontiguousarray(np.asarray(L0_ffn_gate_w, np.float32).T),
        we_gate=np.ascontiguousarray(L0_ffn_w_gate, np.float32),
        we_up=np.ascontiguousarray(L0_ffn_w_up, np.float32),
        we_down=np.ascontiguousarray(L0_ffn_w_down, np.float32),
        ws_gate=np.ascontiguousarray(L0_ffn_ws_gate, np.float32),
        ws_up=np.ascontiguousarray(L0_ffn_ws_up, np.float32),
        ws_down=np.ascontiguousarray(L0_ffn_ws_down, np.float32),
        **consts,
    )

    if FUSED:
        return kernel_fused(wmap, x)

    nc1 = build_p0_program()
    in1 = []
    for cid in range(NCORE):
        b, h = cid // 2, cid % 2
        m = dict(wmap)
        m["x"] = np.ascontiguousarray(x[b, h * TOKC:(h + 1) * TOKC, :])
        in1.append(m)
    r1 = run_bass_kernel_spmd(nc1, in1, core_ids=list(range(NCORE)))
    x1 = [r1.results[cid]["x1"] for cid in range(NCORE)]
    qkv = [r1.results[cid]["qkv"] for cid in range(NCORE)]
    wcg = r1.results[0]["wc_g"]
    wcu = r1.results[0]["wc_u"]
    wcd = r1.results[0]["wc_d"]

    nc2 = build_scan_program()
    in2 = []
    for cid in range(NCORE):
        b, hh = cid // 2, cid % 2
        qa, qb = qkv[2 * b], qkv[2 * b + 1]
        rows = slice(hh * 256, (hh + 1) * 256)
        me = np.empty((NCH, 768, L), qa.dtype)
        for j in range(3):
            full = np.concatenate([qa[j][rows], qb[j][rows]], axis=1)
            me[:, j * 256:(j + 1) * 256, :] = (
                full.reshape(256, NCH, L).transpose(1, 0, 2))
        in2.append({k: wmap[k] for k in wmap} | {"qkv_me": me})
    r2 = run_bass_kernel_spmd(nc2, in2, core_ids=list(range(NCORE)))

    nc3 = build_p2_program()
    in3 = []
    for cid in range(NCORE):
        b, h = cid // 2, cid % 2
        o0 = r2.results[2 * b]["o_me"]
        o1 = r2.results[2 * b + 1]["o_me"]
        ofull = np.concatenate([
            o0.transpose(1, 0, 2).reshape(256, T),
            o1.transpose(1, 0, 2).reshape(256, T)], axis=0)
        m = dict(wmap)
        m["x1"] = x1[cid]
        m["o_mine"] = np.ascontiguousarray(ofull[:, h * TOKC:(h + 1) * TOKC])
        m["wc_g"], m["wc_u"], m["wc_d"] = wcg, wcu, wcd
        in3.append(m)
    r3 = run_bass_kernel_spmd(nc3, in3, core_ids=list(range(NCORE)))

    y = np.empty((B, T, D), np.float32)
    for cid in range(NCORE):
        b, h = cid // 2, cid % 2
        y[b, h * TOKC:(h + 1) * TOKC, :] = r3.results[cid]["y"]
    return y


class DualInc:
    def __init__(self, *insts):
        self.insts = insts

    def then_inc(self, sem, val):
        for i in self.insts:
            i.then_inc(sem, val)
        return self


def d3dyn(x3, p=128):
    return x3.rearrange("b (a p) t -> b a p t", p=p).transpose([0, 2, 1, 3])


def build_fused_program():
    nc = make_nc()
    c = Ctx()
    decl_common(nc, c)

    x_in = nc.dram_tensor("x", [TOKC, D], F32, kind="ExternalInput").ap()
    y_out = nc.dram_tensor("y", [TOKC, D], F32, kind="ExternalOutput").ap()
    x1_scr = nc.dram_tensor("x1_scr", [128, NT * 2048], F32, kind="Internal").ap()
    lnsc = nc.dram_tensor("lnsc_s", [128, NT * 2048], BF16, kind="Internal").ap()
    rb_scr = nc.dram_tensor("rb_scr_s", [NT, NE, 128, 256], BF16, kind="Internal").ap()
    rbsc = [rb_scr[t].transpose([1, 0, 2]) for t in range(NT)]
    rbsc_sl = lambda t, e0: rb_scr[t, e0:e0 + EPP].transpose([1, 0, 2])
    wc_g = nc.dram_tensor("wc_g", [NE + 1, 128, WIMG], BF16, kind="Internal").ap()
    wc_u = nc.dram_tensor("wc_u", [NE + 1, 128, WIMG], BF16, kind="Internal").ap()
    wc_d = nc.dram_tensor("wc_d", [NE + 1, 128, WIMGD], BF16, kind="Internal").ap()
    wcache = ([wc_g[i] for i in range(NE + 1)],
              [wc_u[i] for i in range(NE + 1)],
              [wc_d[i] for i in range(NE + 1)])
    shm_qkv = nc.dram_tensor("shm_qkv", [2, NCH, 768, L], BF16,
                             kind="Internal", addr_space="Shared").ap()
    shm_o = nc.dram_tensor("shm_o", [DFF, T], BF16,
                           kind="Internal", addr_space="Shared").ap()

    p = Prog(nc)
    sC = Sem(nc, "sC")

    io0 = Ctx()
    io0.x = x_in
    io0.x1 = x1_scr
    io0.lnsc = lnsc
    io0.rbsc, io0.rbsc_sl = rbsc, rbsc_sl
    io0.wcache = None
    io0.wcache_w = wcache
    io0.cast_weights = True
    io0.sC = sC

    _hh_regs = {}

    def hh_reg(e):
        if id(e) not in _hh_regs:
            _hh_regs[id(e)] = e.to_reg(e.partition_id() % 2)
        return _hh_regs[id(e)]

    def qkv_out(e, t, sem, _fb=None):
        fb = io0._fb

        def arm(hh):
            for j, srcb in enumerate([fb.t1, fb.t2, fb.hh[0]]):
                for m in range(KF):
                    for ch in range(2):
                        e.dma_start(
                            shm_qkv[m // 2, hh * 32 + 2 * t + ch,
                                    j * 256 + (m % 2) * 128: j * 256 + (m % 2) * 128 + 128, :],
                            srcb[:, m * 256 + ch * 128: m * 256 + ch * 128 + 128]
                        ).then_inc(sem.h, 16)
        r = hh_reg(e)
        with e.If_eq(r, 0):
            arm(0)
        with e.Else():
            arm(1)
    io0.qkv_out = qkv_out
    io0.qkv_ndma = 24

    load_consts(p, c, sC)
    with contextlib.ExitStack() as stk:
        fb = FfnBufs(nc, stk)
        io0._fb = fb
        emit_ffn_phase(p, c, fb, io0, "p0")
        p.emit("p0")
    nc.all_core_barrier()

    ios = Ctx()
    ios.sC = sC

    def qkv_in(e, ch, qc, qf, sem):
        r = hh_reg(e)

        def arm(hh):
            blk = shm_qkv[hh, ch]
            e.dma_start(s3(qc, 6), d3(blk)).then_inc(sem.h, 16)
            e.dma_start(s3(qf, 24), d3(blk, p=32)).then_inc(sem.h, 16)
        with e.If_eq(r, 0):
            arm(0)
        with e.Else():
            arm(1)
    ios.qkv_in = qkv_in

    def o_out(e, ch, ofm, sem):
        r = hh_reg(e)

        def arm(hh):
            for s in range(2):
                e.dma_start(
                    shm_o[hh * 256 + s * 128: hh * 256 + (s + 1) * 128,
                          ch * L:(ch + 1) * L],
                    ofm[:, s * 128:(s + 1) * 128]).then_inc(sem.h, 16)
        with e.If_eq(r, 0):
            arm(0)
        with e.Else():
            arm(1)
    ios.o_out = o_out
    ios.o_ndma = 2

    emit_scan(p, c, ios)
    p.emit("scan")
    nc.all_core_barrier()

    io2 = Ctx()
    io2.x1 = x1_scr
    io2.lnsc = lnsc
    io2.rbsc, io2.rbsc_sl = rbsc, rbsc_sl
    io2.y = y_out
    io2.wcache = wcache
    io2.cast_weights = False
    io2.sC = sC

    def o_load(e, t):
        fb2 = io2._fb
        r = hh_reg(e)
        with e.If_eq(r, 0):
            i1 = e.dma_start(s3(fb2.o_in, KF),
                             d3(shm_o[:, t * TT:(t + 1) * TT]))
        with e.Else():
            i2 = e.dma_start(s3(fb2.o_in, KF),
                             d3(shm_o[:, TOKC + t * TT: TOKC + (t + 1) * TT]))
        return DualInc(i1, i2)
    io2.o_load = o_load

    with contextlib.ExitStack() as stk:
        fb2 = FfnBufs(nc, stk)
        io2._fb = fb2
        emit_ffn_phase(p, c, fb2, io2, "p2")
        p.emit("p2")
    return nc


FUSED = True


def kernel_fused(wmap, x):
    nc = build_fused_program()
    ins = []
    for cid in range(NCORE):
        b, h = cid // 2, cid % 2
        m = dict(wmap)
        m["x"] = np.ascontiguousarray(x[b, h * TOKC:(h + 1) * TOKC, :])
        ins.append(m)
    res = run_bass_kernel_spmd(nc, ins, core_ids=list(range(NCORE)))
    y = np.empty((B, T, D), np.float32)
    for cid in range(NCORE):
        b, h = cid // 2, cid % 2
        y[b, h * TOKC:(h + 1) * TOKC, :] = res.results[cid]["y"]
    return y


# version 17 (final)
# speedup vs baseline: 1.0294x; 1.0294x over previous
import contextlib
import numpy as np
import concourse.bass as bass
import concourse.mybir as mybir
from concourse.bass_utils import run_bass_kernel_spmd

F32 = mybir.dt.float32
BF16 = mybir.dt.bfloat16
AF = mybir.ActivationFunctionType
ALU = mybir.AluOpType
ds = bass.ds

B, T, D = 4, 8192, 1024
NCORE = 8
TOKC = T // 2
TT = 256
NT = TOKC // TT
DFF = 512
NE = 16
NPASS = 4
EPP = NE // NPASS
GAMMA = 0.99
H, DK, DV = 16, 32, 32
HPC = 8
L = 128
NCH = T // L
GL = float(GAMMA ** L)
EPS = 1e-5
KD = D // 128
KF = DFF // 128
WIMG = KD * DFF
WIMGD = KF * D


def host_consts():
    ii = np.arange(L, dtype=np.float64)
    mg = np.where(ii[:, None] <= ii[None, :], GAMMA ** (ii[None, :] - ii[:, None]), 0.0)
    m5 = np.where(ii[:, None] < ii[None, :], GAMMA ** (ii[None, :] - ii[:, None]), 0.0)
    maskpack = np.concatenate([m5, mg], axis=1)
    gq5 = np.broadcast_to(GAMMA ** (ii + 1), (128, L)).copy()
    gcols = np.zeros((128, 4))
    gcols[:, 0] = GAMMA ** (ii + 1)
    gcols[:, 1] = -(GAMMA ** (ii + 1))
    gcols[:, 2] = GAMMA ** (L - 1 - ii)
    gcols[:, 3] = EPS
    sel = np.zeros((16, 16 * 128))
    for e in range(16):
        sel[e, e * 128:(e + 1) * 128] = 1.0
    return {
        "c_id128": np.eye(128, dtype=np.float32),
        "c_ones_k": np.ones((128, 1), np.float32),
        "c_ones_r": np.ones((1, 128), np.float32),
        "c_sel": sel.astype(np.float32),
        "c_maskpack": maskpack.astype(np.float32),
        "c_gq5": gq5.astype(np.float32),
        "c_gcols": gcols.astype(np.float32),
    }


CONST_SHAPES = {
    "c_id128": [128, 128], "c_ones_k": [128, 1], "c_ones_r": [1, 128],
    "c_sel": [16, 2048], "c_maskpack": [128, 256], "c_gq5": [128, 128],
    "c_gcols": [128, 4],
}


DEBUG = False

_UNIQ = [0]


def uniq(name):
    _UNIQ[0] += 1
    return f"{name}_{_UNIQ[0]}"


class Sem:
    def __init__(self, nc, name):
        self.h = nc.alloc_semaphore(uniq(name))
        self.n = 0


class Prog:
    ENGS = ("sync", "gpsimd", "scalar", "vector", "tensor")

    def __init__(self, nc):
        self.nc = nc
        self.th = {e: [] for e in self.ENGS}
        self._nseq = 0
        self.sq = Sem(nc, "sq0")
        self.prev_chain = None
        self._qs = {}
        self._pending = []

    def qsem(self, eng):
        if eng not in self._qs:
            self._qs[eng] = Sem(self.nc, f"q_{eng}")
        return self._qs[eng]

    def new_chain(self):
        self.prev_chain = (self.sq, self.sq.n)
        self._nseq += 1
        self.sq = Sem(self.nc, f"sq{self._nseq}")

    def step(self, eng, fn, dma=False, extra_waits=()):
        ew = list(extra_waits) + self._pending
        self._pending = []
        tgt, sem = self.sq.n, self.sq
        if dma:
            qs = self.qsem(eng)
            qs.n += 16
            qtgt = qs.n

            def th(e, _fn=fn, _t=tgt, _s=sem, _qs=qs, _ew=tuple(ew)):
                for (s2, v2) in _ew:
                    if v2 > 0:
                        e.wait_ge(s2.h, v2)
                if _t > 0:
                    e.wait_ge(_s.h, _t)
                _fn(e).then_inc(_qs.h, 16)
            self.th[eng].append(th)
            self._pending.append((qs, qtgt))
        else:
            def th(e, _fn=fn, _t=tgt, _s=sem, _ew=tuple(ew)):
                for (s2, v2) in _ew:
                    if v2 > 0:
                        e.wait_ge(s2.h, v2)
                if _t > 0:
                    e.wait_ge(_s.h, _t)
                _fn(e).then_inc(_s.h, 1)
            self.th[eng].append(th)
            self.sq.n += 1

    def raw(self, eng, fn):
        self.th[eng].append(fn)

    def issue(self, eng, fn, extra_waits=(), chain_wait=True):
        tgt, sem, ew = self.sq.n, self.sq, tuple(extra_waits)

        def th(e, _fn=fn, _t=tgt if chain_wait else 0, _s=sem, _ew=ew):
            for (s2, v2) in _ew:
                if v2 > 0:
                    e.wait_ge(s2.h, v2)
            if _t > 0:
                e.wait_ge(_s.h, _t)
            _fn(e)
        self.th[eng].append(th)

    def emit(self, name="blk"):
        nc, th = self.nc, self.th
        with nc.Block(name) as block:
            @block.sync
            def _(e):
                for f in th["sync"]:
                    f(e)

            @block.gpsimd
            def _(e):
                for f in th["gpsimd"]:
                    f(e)

            @block.scalar
            def _(e):
                for f in th["scalar"]:
                    f(e)

            @block.vector
            def _(e):
                for f in th["vector"]:
                    f(e)

            @block.tensor
            def _(e):
                for f in th["tensor"]:
                    f(e)
        self.th = {e: [] for e in self.ENGS}


class Ctx:
    pass


def make_nc():
    return bass.Bass("TRN2", target_bir_lowering=False, num_devices=NCORE)


def d3(x2d, p=128):
    return x2d.rearrange("(a p) n -> a p n", p=p).transpose([1, 0, 2])


def s3(sb2d, a):
    return sb2d.rearrange("p (a n) -> p a n", a=a)


def w_ap(w):
    return d3(w)


def w3_ap(w, e):
    return d3(w[e])


def decl_common(nc, c):
    c.wq = nc.dram_tensor("wq", [D, DFF], F32, kind="ExternalInput").ap()
    c.wk = nc.dram_tensor("wk", [D, DFF], F32, kind="ExternalInput").ap()
    c.wv = nc.dram_tensor("wv", [D, DFF], F32, kind="ExternalInput").ap()
    c.wo = nc.dram_tensor("wo", [DFF, D], F32, kind="ExternalInput").ap()
    c.gate_wT = nc.dram_tensor("gate_wT", [D, NE], F32, kind="ExternalInput").ap()
    c.we_gate = nc.dram_tensor("we_gate", [NE, D, DFF], F32, kind="ExternalInput").ap()
    c.we_up = nc.dram_tensor("we_up", [NE, D, DFF], F32, kind="ExternalInput").ap()
    c.we_down = nc.dram_tensor("we_down", [NE, DFF, D], F32, kind="ExternalInput").ap()
    c.ws_gate = nc.dram_tensor("ws_gate", [D, DFF], F32, kind="ExternalInput").ap()
    c.ws_up = nc.dram_tensor("ws_up", [D, DFF], F32, kind="ExternalInput").ap()
    c.ws_down = nc.dram_tensor("ws_down", [DFF, D], F32, kind="ExternalInput").ap()
    for n, shp in CONST_SHAPES.items():
        setattr(c, n, nc.dram_tensor(n, shp, F32, kind="ExternalInput").ap())
    c.id_f = nc.alloc_sbuf_tensor("id_f", [128, 128], F32).ap()
    c.id_b = nc.alloc_sbuf_tensor("id_b", [128, 128], BF16).ap()
    c.ones_k = nc.alloc_sbuf_tensor("ones_k", [128, 1], BF16).ap()
    c.ones_r = nc.alloc_sbuf_tensor("ones_r", [1, 128], F32).ap()
    c.sel_b = nc.alloc_sbuf_tensor("sel_b", [16, 2048], BF16).ap()
    c.maskp = nc.alloc_sbuf_tensor("maskp", [128, 256], F32).ap()
    c.gq5 = nc.alloc_sbuf_tensor("gq5", [128, 128], F32).ap()
    c.gcols = nc.alloc_sbuf_tensor("gcols", [128, 4], F32).ap()
    c.gatewT = nc.alloc_sbuf_tensor("gatewT", [128, KD * NE], F32).ap()


def load_consts(p, c, sC):
    def f(e):
        e.dma_start(c.id_f, c.c_id128).then_inc(sC.h, 16)
        e.dma_start(c.id_b, c.c_id128).then_inc(sC.h, 16)
        e.dma_start(c.ones_k, c.c_ones_k).then_inc(sC.h, 16)
        e.dma_start(c.ones_r, c.c_ones_r).then_inc(sC.h, 16)
        e.dma_start(c.sel_b, c.c_sel).then_inc(sC.h, 16)
        e.dma_start(c.maskp, c.c_maskpack).then_inc(sC.h, 16)
        e.dma_start(c.gq5, c.c_gq5).then_inc(sC.h, 16)
        e.dma_start(c.gcols, c.c_gcols).then_inc(sC.h, 16)
        e.dma_start(s3(c.gatewT, KD), d3(c.gate_wT)).then_inc(sC.h, 16)
    p.th["gpsimd"].append(f)
    sC.n += 9 * 16


class FfnBufs:
    def __init__(self, nc, stk):
        A = lambda n, s, d: stk.enter_context(nc.sbuf_tensor(uniq(n), s, d)).ap()
        self.wg = [A(f"wg{i}", [128, WIMG], BF16) for i in range(5)]
        self.wu = [A(f"wu{i}", [128, WIMG], BF16) for i in range(5)]
        self.wd = [A(f"wd{i}", [128, WIMGD], BF16) for i in range(5)]
        self.x_in = [A("x_in0", [128, 2048], F32), A("x_in1", [128, 2048], F32)]
        self.x_fm = A("x_fm", [128, 2048], F32)
        self.ln_f = A("ln_f", [128, 2048], F32)
        self.ln_b = [A("ln_b0", [128, 2048], BF16), A("ln_b1", [128, 2048], BF16)]
        self.rbs = [A("rbs0", [128, 1280], BF16), A("rbs1", [128, 1280], BF16)]
        self.scr = A("scr", [128, 4096], BF16)
        self.hh = [A("hh0", [128, 1024], BF16), A("hh1", [128, 1024], BF16)]
        self.xout = [A(f"xout{i}", [128, 2048], F32) for i in range(2)]
        self.stat = A("stat", [1, 1024], F32)
        self.scores = A("scores", [128, 32], F32)
        self.maxes = A("maxes", [128, 16], F32)
        self.rsm = A("rsm", [128, 16], F32)
        self.istmp = A("istmp", [128, 32], F32)
        self.route = A("route", [128, 32], F32)
        self.rt4 = A("rt4", [16, 1024], BF16)
        self.x_bf = self.scr[:, 0:2048]
        self.sq_bf = self.scr[:, 2048:4096]
        self.t1 = self.scr[:, 0:1024]
        self.t2 = self.scr[:, 1024:2048]
        self.u_sb = self.scr[:, 2048:3072]
        self.o_in = self.wg[4][:, 0:1024]


class FfnPsum:
    def __init__(self, nc, stk):
        E = stk.enter_context
        self.g = E(nc.psum_tensor(uniq("pg"), [128, KF * TT], F32)).ap()
        self.u = E(nc.psum_tensor(uniq("pu"), [128, KF * TT], F32)).ap()
        self.yacc = E(nc.psum_tensor(uniq("pyacc"), [128, KD * TT], F32)).ap()
        self.misc = self.g[:, 0:512]
        self.rbst = self.u[:, 0:512]


def emit_ln(p, c, fb, pp, src, lnb):
    p.step("vector", lambda e: e.tensor_copy(fb.x_bf, src[:, 0:2048]))
    p.step("scalar", lambda e: e.activation(fb.sq_bf, src[:, 0:2048], AF.Square))

    def f(e):
        for k in range(KD):
            e.matmul(pp.misc[0:1, 0:256], c.ones_k,
                     fb.x_bf[:, k * 256:(k + 1) * 256],
                     start=(k == 0), stop=(k == KD - 1))
        for k in range(KD):
            last = e.matmul(pp.misc[0:1, 256:512], c.ones_k,
                            fb.sq_bf[:, k * 256:(k + 1) * 256],
                            start=(k == 0), stop=(k == KD - 1))
        return last
    p.step("tensor", f)

    def f(e):
        e.tensor_scalar_mul(fb.stat[:, 0:256], pp.misc[0:1, 0:256], 1.0 / D)
        e.tensor_scalar_mul(fb.stat[:, 256:512], pp.misc[0:1, 256:512], 1.0 / D)
        e.drain()
        e.tensor_mul(fb.stat[:, 512:768], fb.stat[:, 0:256], fb.stat[:, 0:256])
        e.drain()
        return e.tensor_sub(fb.stat[:, 512:768], fb.stat[:, 256:512],
                            fb.stat[:, 512:768])
    p.step("vector", f)
    p.step("scalar", lambda e: e.activation(fb.stat[:, 256:512], fb.stat[:, 512:768],
                                            AF.Sqrt, bias=c.gcols[0:1, 3:4]))

    def f(e):
        e.reciprocal(fb.stat[:, 512:768], fb.stat[:, 256:512])
        e.drain()
        return e.scalar_tensor_tensor(fb.stat[:, 768:1024], fb.stat[:, 0:256], -1.0,
                                      fb.stat[:, 512:768], ALU.mult, ALU.mult)
    p.step("vector", f)

    def f(e):
        e.matmul(pp.misc[:, 0:256], c.ones_r, fb.stat[:, 512:768],
                 start=True, stop=True)
        return e.matmul(pp.misc[:, 256:512], c.ones_r, fb.stat[:, 768:1024],
                        start=True, stop=True)
    p.step("tensor", f)

    def f(e):
        for k in range(KD):
            sl = slice(k * 256, (k + 1) * 256)
            e.tensor_mul(fb.ln_f[:, sl], src[:, sl], pp.misc[:, 0:256])
            e.drain()
            e.tensor_add(fb.ln_f[:, sl], fb.ln_f[:, sl], pp.misc[:, 256:512])
        e.drain()
        return e.tensor_copy(lnb[:], fb.ln_f[:])
    p.step("vector", f)


def emit_route(p, c, fb, pp):
    def f(e):
        for m in range(2):
            for k in range(KD):
                last = e.matmul(pp.misc[:, m * 16:(m + 1) * 16],
                                fb.ln_f[:, k * 256 + m * 128: k * 256 + m * 128 + 128],
                                c.gatewT[:, k * 16:(k + 1) * 16],
                                start=(k == 0), stop=(k == KD - 1))
        return last
    p.step("tensor", f)
    p.step("scalar", lambda e: e.activation(fb.scores[:], pp.misc[:, 0:32], AF.Sigmoid))

    def f(e):
        for m in range(2):
            sc = fb.scores[:, m * 16:(m + 1) * 16]
            mx = fb.maxes[:, m * 8:(m + 1) * 8]
            e.max(out=mx, in_=sc)
            e.drain()
            r = fb.rsm[:, m * 8:(m + 1) * 8]
            m1, m2 = mx[:, 0:1], mx[:, 1:2]
            den, rec, w1, w2 = r[:, 0:1], r[:, 1:2], r[:, 2:3], r[:, 3:4]
            e.tensor_scalar(den, m1, m2, 1e-20, ALU.add, ALU.add)
            e.drain()
            e.reciprocal(rec, den)
            e.drain()
            e.tensor_scalar(w1, m1, rec, 1.25, ALU.mult, ALU.mult)
            e.tensor_scalar(w2, m2, rec, 1.25, ALU.mult, ALU.mult)
            e.drain()
            rt = fb.route[:, m * 16:(m + 1) * 16]
            it = fb.istmp[:, m * 16:(m + 1) * 16]
            e.tensor_single_scalar(it, sc, m1, ALU.is_equal)
            e.drain()
            e.tensor_scalar_mul(rt, it, w1)
            e.drain()
            e.tensor_single_scalar(it, sc, m2, ALU.is_equal)
            e.drain()
            last = e.scalar_tensor_tensor(rt, it, w2, rt, ALU.mult, ALU.add)
            e.drain()
        return last
    p.step("vector", f)

    def f(e):
        e.transpose(pp.misc[0:16, 0:128], fb.route[:, 0:16], c.id_f)
        return e.transpose(pp.misc[0:16, 128:256], fb.route[:, 16:32], c.id_f)
    p.step("tensor", f)

    def f(e):
        for r in range(4):
            last = e.tensor_copy(fb.rt4[:, r * 256:(r + 1) * 256],
                                 pp.misc[0:16, 0:256])
        return last
    p.step("vector", f)


def emit_rb_all(p, c, fb, pp):
    for eid in range(NE):
        def fmm(e, _e=eid):
            return e.matmul(pp.rbst[:, 0:256] if _e % 2 == 0 else pp.rbst[:, 256:512],
                            c.sel_b[:, _e * 128:(_e + 1) * 128],
                            fb.rt4[:, 0:256], start=True, stop=True)
        p.step("tensor", fmm)

        def fcp(e, _e=eid):
            return e.copy(fb.scr[:, _e * 256:(_e + 1) * 256],
                          pp.rbst[:, 0:256] if _e % 2 == 0 else pp.rbst[:, 256:512])
        p.step("scalar", fcp)


class ESems:
    def __init__(self, nc, tag):
        self.gu = Sem(nc, f"eGU_{tag}")
        self.si = Sem(nc, f"eSI_{tag}")
        self.uc = Sem(nc, f"eUC_{tag}")
        self.h = Sem(nc, f"eH_{tag}")
        self.dn = Sem(nc, f"eDN_{tag}")


def rb_bcast(rbs_par, sl):
    return rbs_par[:, sl * 256:(sl + 1) * 256] \
        .rearrange("p (o n) -> p o n", o=1).broadcast_to([128, KF, 256])


def emit_experts(p, c, fb, pp, slots, es, lnb, first_waits=(), dn_wait=None,
                 si_first_waits=(), h_first_waits=()):
    nslots = len(slots)
    chain0 = None
    pend = list(p._pending) + list(first_waits)
    p._pending = []

    def dn_of(i, wd, first, last_):
        def fd(e, _wd=wd, _first=first, _last=last_, _q=i % 2):
            for m in range(KD):
                for k in range(KF):
                    last = e.matmul(pp.yacc[:, m * TT:(m + 1) * TT],
                                    _wd[:, k * D + m * 128: k * D + m * 128 + 128],
                                    fb.hh[_q][:, k * TT:(k + 1) * TT],
                                    start=(_first and k == 0 and m % 2 == 0),
                                    stop=(_last and k == KF - 1),
                                    skip_group_check=True)
            return last
        return fd

    pend_dn = None
    for i, (wg, wu, wd, rb) in enumerate(slots):
        w_si, w_uc = es.si.n, es.uc.n
        cw = chain0 if i == 0 else None

        def fgu(e, _wg=wg, _wu=wu, _wsi=w_si, _wuc=w_uc, _cw=cw,
                _pend=tuple(pend) if i == 0 else ()):
            for (s2, v2) in _pend:
                if v2 > 0:
                    e.wait_ge(s2.h, v2)
            if _cw is not None and _cw[1] > 0:
                e.wait_ge(_cw[0].h, _cw[1])
            if _wsi > 0:
                e.wait_ge(es.si.h, _wsi)
            if _wuc > 0:
                e.wait_ge(es.uc.h, _wuc)
            for m in range(KF):
                for k in range(KD):
                    e.matmul(pp.g[:, m * TT:(m + 1) * TT],
                             _wg[:, k * DFF + m * 128: k * DFF + m * 128 + 128],
                             lnb[:, k * TT:(k + 1) * TT],
                             start=(k == 0), stop=(k == KD - 1))
            for m in range(KF):
                for k in range(KD):
                    last = e.matmul(pp.u[:, m * TT:(m + 1) * TT],
                                    _wu[:, k * DFF + m * 128: k * DFF + m * 128 + 128],
                                    lnb[:, k * TT:(k + 1) * TT],
                                    start=(k == 0), stop=(k == KD - 1))
            last.then_inc(es.gu.h, 1)
            return last
        p.raw("tensor", fgu)
        es.gu.n += 1

        w_gu, w_h2 = es.gu.n, es.h.n

        def fsi(e, _wgu=w_gu, _wh=w_h2,
                _fw=tuple(si_first_waits) if i == 0 else ()):
            for (s2, v2) in _fw:
                if v2 > 0:
                    e.wait_ge(s2.h, v2)
            e.wait_ge(es.gu.h, _wgu)
            if _wh > 0:
                e.wait_ge(es.h.h, _wh)
            return e.activation(fb.t1, pp.g[:], AF.Silu).then_inc(es.si.h, 1)
        p.raw("scalar", fsi)
        es.si.n += 1

        w_gu2 = es.gu.n

        def fuc(e, _wgu=w_gu2,
                _fw=tuple(h_first_waits) if i == 0 else ()):
            for (s2, v2) in _fw:
                if v2 > 0:
                    e.wait_ge(s2.h, v2)
            e.wait_ge(es.gu.h, _wgu)
            return e.tensor_copy(fb.u_sb, pp.u[:]).then_inc(es.uc.h, 1)
        p.raw("vector", fuc)
        es.uc.n += 1

        w_si2, w_dn = es.si.n, es.dn.n

        def fh(e, _rb=rb, _wsi=w_si2, _wdn=w_dn, _q=i % 2):
            e.wait_ge(es.si.h, _wsi)
            if _wdn > 0:
                e.wait_ge(es.dn.h, _wdn)
            e.drain()
            e.tensor_mul(fb.t2, fb.t1, fb.u_sb)
            e.drain()
            if _rb is None:
                last = e.tensor_scalar_mul(fb.hh[_q][:], fb.t2, 0.5)
            else:
                last = e.tensor_mul(s3(fb.hh[_q], KF), s3(fb.t2, KF), _rb)
            last.then_inc(es.h.h, 1)
            return last
        p.raw("vector", fh)
        es.h.n += 1

        if pend_dn is not None:
            fd_prev, w_h_prev, dnw = pend_dn

            def fdp(e, _fd=fd_prev, _wh=w_h_prev, _dnw=dnw):
                if _dnw is not None and _dnw[1] > 0:
                    e.wait_ge(_dnw[0].h, _dnw[1])
                e.wait_ge(es.h.h, _wh)
                _fd(e).then_inc(es.dn.h, 1)
            p.raw("tensor", fdp)
            es.dn.n += 1
        pend_dn = (dn_of(i, wd, i == 0, i == nslots - 1), es.h.n,
                   dn_wait if i == 0 else None)

    fd_last, w_h_last, dnw = pend_dn

    def fdl(e, _fd=fd_last, _wh=w_h_last, _dnw=dnw):
        if _dnw is not None and _dnw[1] > 0:
            e.wait_ge(_dnw[0].h, _dnw[1])
        e.wait_ge(es.h.h, _wh)
        _fd(e).then_inc(es.dn.h, 1)
    p.raw("tensor", fdl)
    es.dn.n += 1


def emit_ffn_phase(p, c, fb, io, phase):
    nc = p.nc
    sW = Sem(nc, f"sW_{phase}")
    sOUTg = Sem(nc, f"sOUTg_{phase}")
    sOUTs = Sem(nc, f"sOUTs_{phase}")
    sINs = [Sem(nc, f"sINs{i}_{phase}") for i in range(2)]
    sINg = [Sem(nc, f"sINg{i}_{phase}") for i in range(2)]
    stk_ph = contextlib.ExitStack()
    pp = FfnPsum(nc, stk_ph)
    es = ESems(nc, phase)

    p.new_chain()
    pcL = p.prev_chain
    if phase == "p2":
        def fwo(e, _pc=pcL):
            if _pc is not None and _pc[1] > 0:
                e.wait_ge(_pc[0].h, _pc[1])
            e.dma_start(s3(fb.wd[4], KF), w_ap(c.wo)).then_inc(sW.h, 16)
        p.th["gpsimd"].append(fwo)
        sW.n += 16
    wo_tgt = sW.n

    if io.cast_weights:
        sWc = [Sem(nc, f"sWc{j}_p0") for j in range(3)]
        sWcw = [Sem(nc, f"sWcw{j}_p0") for j in range(3)]
        srcs3 = [lambda e_: w3_ap(c.we_gate, e_), lambda e_: w3_ap(c.we_up, e_),
                 lambda e_: w3_ap(c.we_down, e_)]
        stg = [fb.wg[4], fb.wu[4], fb.wd[4]]
        kds = [KD, KD, KF]

        def fcast_w(e, _pc=pcL):
            if _pc is not None and _pc[1] > 0:
                e.wait_ge(_pc[0].h, _pc[1])
            for eid in range(NE + 1):
                for j in range(3):
                    if eid >= 1:
                        e.wait_ge(sWcw[j].h, eid * 16)
                    if eid <= NE - 1:
                        e.dma_start(s3(stg[j], kds[j]), srcs3[j](eid)
                                    ).then_inc(sWc[j].h, 16)
                    else:
                        wsrc = [c.ws_gate, c.ws_up, c.ws_down][j]
                        e.dma_start(s3(stg[j], kds[j]), w_ap(wsrc)
                                    ).then_inc(sWc[j].h, 16)
        p.th["gpsimd"].append(fcast_w)

        def fwrite_w(e):
            for eid in range(NE + 1):
                for j in range(3):
                    e.wait_ge(sWc[j].h, (eid + 1) * 16)
                    e.dma_start(io.wcache_w[j][eid], stg[j][:]
                                ).then_inc(sWcw[j].h, 16)
        p.th["sync"].append(fwrite_w)
        for j in range(3):
            sWc[j].n = 16 * (NE + 1)
            sWcw[j].n = 16 * (NE + 1)
        cache_ready = 16 * (NE + 1)
    else:
        sWcw, cache_ready = None, 0

    for t in range(NT):
        ew = [(sOUTg, sOUTg.n), (sOUTs, sOUTs.n)]
        if t == 0:
            ew.append((io.sC, io.sC.n))
            ew.append((sW, wo_tgt))
        ewx = ew
        pc_extra = pcL if t == 0 else None

        if phase == "p0":
            def fx(e, _t=t, _pc=pc_extra, _ew=tuple(ewx)):
                for (s2, v2) in _ew:
                    if v2 > 0:
                        e.wait_ge(s2.h, v2)
                if _pc is not None and _pc[1] > 0:
                    e.wait_ge(_pc[0].h, _pc[1])
                return e.dma_start(s3(fb.x_in[0], 2),
                                   d3(io.x[_t * TT:(_t + 1) * TT, :]))
            p.step("sync", fx, dma=True)

            def ftr(e):
                for s in range(2):
                    for fc in range(KD):
                        last = e.transpose(
                            pp.yacc[:, (fc * 2 + s) * 128:(fc * 2 + s) * 128 + 128],
                            fb.x_in[0][:, s * 1024 + fc * 128: s * 1024 + fc * 128 + 128],
                            c.id_f)
                return last
            p.step("tensor", ftr)
            p.step("vector", lambda e: e.tensor_copy(fb.x_fm[:], pp.yacc[:]))
        else:
            def fx(e, _t=t, _pc=pc_extra, _ew=tuple(ewx)):
                for (s2, v2) in _ew:
                    if v2 > 0:
                        e.wait_ge(s2.h, v2)
                if _pc is not None and _pc[1] > 0:
                    e.wait_ge(_pc[0].h, _pc[1])
                return e.dma_start(fb.x_in[0][:],
                                   io.x1[:, _t * 2048:(_t + 1) * 2048])
            p.step("sync", fx, dma=True)

            def fo(e, _t=t):
                return io.o_load(e, _t)
            p.step("gpsimd", fo, dma=True)

            def fatt(e):
                for m in range(KD):
                    for k in range(KF):
                        last = e.matmul(
                            pp.yacc[:, m * TT:(m + 1) * TT],
                            fb.wd[4][:, k * D + m * 128: k * D + m * 128 + 128],
                            fb.o_in[:, k * TT:(k + 1) * TT],
                            start=(k == 0), stop=(k == KF - 1))
                return last
            p.step("tensor", fatt)
            p.step("vector", lambda e: e.tensor_add(fb.x_fm[:], fb.x_in[0][:],
                                                    pp.yacc[:]))

        def fwx(e, _t=t):
            e.dma_start(io.x1[:, _t * 2048:(_t + 1) * 2048],
                        fb.x_fm[:]).then_inc(sOUTs.h, 16)
        p.issue("sync", fwx)
        sOUTs.n += 16

        emit_ln(p, c, fb, pp, fb.x_fm, fb.ln_b[0])
        emit_route(p, c, fb, pp)

        def fsc(e, _t=t):
            e.dma_start(io.lnsc[:, _t * 2048:(_t + 1) * 2048],
                        fb.ln_b[0][:]).then_inc(sOUTg.h, 16)
        p.issue("gpsimd", fsc)
        sOUTg.n += 16

        emit_rb_all(p, c, fb, pp)

        def frb(e, _t=t):
            e.dma_start(io.rbsc[_t], s3(fb.scr[:, 0:NE * 256], NE)).then_inc(sOUTg.h, 16)
        p.issue("gpsimd", frb)
        sOUTg.n += 16

    for pi in range(NPASS):
        p.new_chain()
        pc = p.prev_chain
        el = list(range(pi * EPP, (pi + 1) * EPP))
        with_shared = (pi == 0) if phase == "p0" else (pi == NPASS - 1)
        last_pass = (pi == NPASS - 1)

        wc = io.wcache if io.wcache is not None else io.wcache_w

        def fwl(e, _el=tuple(el), _ws=with_shared, _pi=pi, _pc=pc,
                _sg=sOUTg.n, _ss=sOUTs.n, _wc=wc,
                _cr=cache_ready if (io.cast_weights and pi == 0) else 0):
            if _pc is not None and _pc[1] > 0:
                e.wait_ge(_pc[0].h, _pc[1])
            if _cr > 0:
                for _s in sWcw:
                    e.wait_ge(_s.h, _cr)
            if _sg > 0:
                e.wait_ge(sOUTg.h, _sg)
            if _ss > 0:
                e.wait_ge(sOUTs.h, _ss)
            for sl, eid in enumerate(_el):
                e.dma_start(fb.wg[sl][:], _wc[0][eid]).then_inc(sW.h, 16)
                e.dma_start(fb.wu[sl][:], _wc[1][eid]).then_inc(sW.h, 16)
                e.dma_start(fb.wd[sl][:], _wc[2][eid]).then_inc(sW.h, 16)
            if _ws:
                e.dma_start(fb.wg[4][:], _wc[0][NE]).then_inc(sW.h, 16)
                e.dma_start(fb.wu[4][:], _wc[1][NE]).then_inc(sW.h, 16)
                e.dma_start(fb.wd[4][:], _wc[2][NE]).then_inc(sW.h, 16)
            if phase == "p0" and _pi == NPASS - 1:
                e.dma_start(s3(fb.wg[4], KD), w_ap(c.wq)).then_inc(sW.h, 16)
                e.dma_start(s3(fb.wu[4], KD), w_ap(c.wk)).then_inc(sW.h, 16)
                e.dma_start(s3(fb.wd[4], KD), w_ap(c.wv)).then_inc(sW.h, 16)
        p.th["gpsimd"].append(fwl)
        nw = 3 * len(el) + (3 if with_shared else 0)
        if phase == "p0" and pi == NPASS - 1:
            nw += 3
        sW.n += 16 * nw
        wload_tgt = sW.n

        in_tgt = []

        def load_tile(t, e0=el[0], _pc0=pc):
            par = t % 2
            pcw = _pc0 if t == 0 else None
            outw = (sOUTg.n, sOUTs.n) if t == 0 else None

            def fls(e, _t=t, _par=par, _pc=pcw, _ow=outw):
                if _pc is not None and _pc[1] > 0:
                    e.wait_ge(_pc[0].h, _pc[1])
                if _ow is not None:
                    if _ow[0] > 0:
                        e.wait_ge(sOUTg.h, _ow[0])
                    if _ow[1] > 0:
                        e.wait_ge(sOUTs.h, _ow[1])
                e.dma_start(fb.x_in[_par][:],
                            io.x1[:, _t * 2048:(_t + 1) * 2048]
                            ).then_inc(sINs[_par].h, 16)
            p.issue("sync", fls)
            sINs[par].n += 16

            def flg(e, _t=t, _par=par, _e0=e0, _pc=pcw, _ow=outw):
                if _pc is not None and _pc[1] > 0:
                    e.wait_ge(_pc[0].h, _pc[1])
                if _ow is not None:
                    if _ow[0] > 0:
                        e.wait_ge(sOUTg.h, _ow[0])
                    if _ow[1] > 0:
                        e.wait_ge(sOUTs.h, _ow[1])
                e.dma_start(fb.ln_b[_par][:],
                            io.lnsc[:, _t * 2048:(_t + 1) * 2048]
                            ).then_inc(sINg[_par].h, 16)
                e.dma_start(s3(fb.rbs[_par][:, 0:EPP * 256], EPP),
                            io.rbsc_sl(_t, _e0)).then_inc(sINg[_par].h, 16)
            p.issue("scalar", flg)
            sINg[par].n += 32
            in_tgt.append((sINs[par].n, sINg[par].n))

        load_tile(0)
        for t in range(NT):
            par = t % 2
            if t + 1 < NT:
                load_tile(t + 1)
            ws, wg_ = in_tgt[t]
            ew = [(sINg[par], wg_)]
            if t == 0:
                ew.append((sW, wload_tgt))
            if last_pass and phase == "p0":
                ew.append((p.sq, p.sq.n))
            dn_w = (p.sq, p.sq.n)
            si_w = [(sOUTg, sOUTg.n)]
            h_w = [(sOUTg, sOUTg.n)]

            slots = [(fb.wg[sl], fb.wu[sl], fb.wd[sl], rb_bcast(fb.rbs[par], sl))
                     for sl in range(EPP)]
            if with_shared:
                slots.append((fb.wg[4], fb.wu[4], fb.wd[4], None))
            emit_experts(p, c, fb, pp, slots, es, fb.ln_b[par],
                         first_waits=ew, dn_wait=dn_w,
                         si_first_waits=si_w, h_first_waits=h_w)

            p.step("vector", lambda e, _par=par:
                   e.tensor_add(fb.xout[_par][:], fb.x_in[_par][:, 0:2048], pp.yacc[:]),
                   extra_waits=[(es.dn, es.dn.n), (sINs[par], ws),
                                (sOUTs, sOUTs.n)])

            if not (last_pass and phase == "p2"):
                def fw(e, _t=t, _par=par):
                    e.dma_start(io.x1[:, _t * 2048:(_t + 1) * 2048],
                                fb.xout[_par][:]).then_inc(sOUTs.h, 16)
                p.issue("sync", fw)
                sOUTs.n += 16

            if last_pass and phase == "p0":
                emit_ln(p, c, fb, pp, fb.xout[par], fb.ln_b[par])

                def fq(e, _par=par):
                    dsts = [pp.g, pp.u, pp.yacc]
                    for j in range(3):
                        srcw = [fb.wg[4], fb.wu[4], fb.wd[4]][j]
                        for m in range(KF):
                            for k in range(KD):
                                last = e.matmul(
                                    dsts[j][:, m * TT:(m + 1) * TT],
                                    srcw[:, k * DFF + m * 128: k * DFF + m * 128 + 128],
                                    fb.ln_b[_par][:, k * TT:(k + 1) * TT],
                                    start=(k == 0), stop=(k == KD - 1))
                    return last
                p.step("tensor", fq)

                def fqe(e):
                    e.tensor_copy(fb.t1, pp.g[:])
                    e.tensor_copy(fb.t2, pp.u[:])
                    return e.tensor_copy(fb.hh[0][:], pp.yacc[:, 0:1024])
                p.step("vector", fqe)

                def fqw(e, _t=t):
                    io.qkv_out(e, _t, sOUTg)
                p.issue("gpsimd", fqw)
                sOUTg.n += 16 * io.qkv_ndma

            if last_pass and phase == "p2":
                def ftr2(e, _par=par):
                    for s in range(2):
                        for fc in range(KD):
                            last = e.transpose(
                                pp.yacc[:, (s * KD + fc) * 128:(s * KD + fc) * 128 + 128],
                                fb.xout[_par][:, fc * 256 + s * 128: fc * 256 + s * 128 + 128],
                                c.id_f)
                    return last
                p.step("tensor", ftr2)
                p.step("vector", lambda e: e.tensor_copy(fb.x_fm[:], pp.yacc[:]))

                def fy(e, _t=t):
                    return e.dma_start(
                        d3(io.y[_t * TT:(_t + 1) * TT, :]),
                        s3(fb.x_fm, 2))
                p.step("sync", fy, dma=True)

    stk_ph.close()

    def fdr(e, _g=sOUTg.n, _s=sOUTs.n,
            _qg=p.qsem("gpsimd").n, _qs=p.qsem("sync").n):
        if _g:
            e.wait_ge(sOUTg.h, _g)
        if _s:
            e.wait_ge(sOUTs.h, _s)
        if _qg:
            e.wait_ge(p.qsem("gpsimd").h, _qg)
        if _qs:
            e.wait_ge(p.qsem("sync").h, _qs)
        return e.nop()
    p.step("gpsimd", fdr)
    return sOUTg


NSET = 3


class ScanBufs:
    def __init__(self, nc, stk):
        A = lambda n, s, d: stk.enter_context(nc.sbuf_tensor(uniq(n), s, d)).ap()
        self.qc = [A(f"qc{i}", [128, 768], BF16) for i in range(2)]
        self.qf = [A(f"qf{i}", [32, 3072], BF16) for i in range(2)]
        self.tok = A("tok", [128, 768], BF16)
        self.q5 = A("q5", [32, 1024], BF16)
        self.kw = A("kw", [128, 256], BF16)
        self.qw = A("qw", [128, 256], BF16)
        self.scg = A("scg", [32, 768], F32)
        self.scgb = A("scgb", [32, 768], BF16)
        self.aw = [A(f"awsc{i}", [128, 448], BF16) for i in range(NSET)]
        self.bm = [A(f"bm{i}", [128, 256], BF16) for i in range(NSET)]
        self.fm = [A(f"fmsc{i}", [128, 128], BF16) for i in range(NSET)]
        self.dm = [A(f"dmsc{i}", [128, 128], BF16) for i in range(NSET)]
        self.kcn = [A(f"kcn{i}", [128, 32], BF16) for i in range(NSET)]
        self.obt = [A(f"obt{i}", [128, 32], F32) for i in range(NSET)]
        self.osb = A("osb", [128, 256], BF16)
        self.ofm = A("ofm", [128, 256], BF16)
        self.gq5b = A("gq5b", [32, 128], F32)


class ScanPsum:
    def __init__(self, nc, stk):
        E = stk.enter_context
        self.aq = [E(nc.psum_tensor(uniq("saq"), [128, 512], F32)).ap()
                   for _ in range(NSET)]
        self.fdq = [E(nc.psum_tensor(uniq("sfdq"), [128, 512], F32)).ap()
                    for _ in range(NSET)]
        tb = E(nc.psum_tensor(uniq("stok"), [128, 1024], BF16)).ap()
        self.tokp = tb[:, 0:768]
        self.ofmp = tb[:, 768:1024]


class SSems:
    def __init__(self, nc):
        for n in ("t1", "s2", "v3", "t4", "v5", "t6", "v7", "t8", "v10"):
            setattr(self, n, Sem(nc, f"sc_{n}"))


def emit_scan(p, c, io):
    nc = p.nc
    p.new_chain()
    pc = p.prev_chain
    sX = [Sem(nc, "sX_scan0"), Sem(nc, "sX_scan1")]
    sOUT = Sem(nc, "sOUT_scan")
    ss = SSems(nc)
    with contextlib.ExitStack() as stk:
        sb = ScanBufs(nc, stk)
        ps = ScanPsum(nc, stk)

        def fz(e, _pc=pc):
            if _pc is not None and _pc[1] > 0:
                e.wait_ge(_pc[0].h, _pc[1])
            e.memset(sb.scg[:], 0.0)
            for s in range(NSET):
                e.memset(ps.aq[s][:], 0.0)
                e.memset(ps.fdq[s][:], 0.0)
            return e.memset(sb.gq5b[:], 0.0)
        p.step("vector", fz, extra_waits=[(io.sC, io.sC.n)])
        p.step("vector", lambda e: e.tensor_copy(sb.gq5b[:], c.gq5[0:32, :]))

        gpair = 0
        for ch in range(NCH):
            par = ch % 2

            def fqd(e, _ch=ch, _par=par, _t8=ss.t8.n):
                if _t8 > 0:
                    e.wait_ge(ss.t8.h, _t8)
                io.qkv_in(e, _ch, sb.qc[_par], sb.qf[_par], sX[_par])
            p.issue("gpsimd", fqd, chain_wait=False)
            sX[par].n += 32

            def fcast(e, _v10=ss.v10.n, _o=sOUT.n):
                e.wait_ge(ss.v10.h, _v10) if _v10 > 0 else None
                if _o > 0:
                    e.wait_ge(sOUT.h, _o)
                return e.tensor_copy(sb.scgb[:], sb.scg[:])
            p.step("vector", fcast, extra_waits=[(sX[par], sX[par].n)])

            qs, qf = sb.qc[par], sb.qf[par]

            def ftr(e, _qs=qs):
                for s in range(6):
                    last = e.transpose(ps.tokp[:, s * 128:(s + 1) * 128],
                                       _qs[:, s * 128:(s + 1) * 128], c.id_b)
                return last
            p.step("tensor", ftr)
            p.step("scalar", lambda e: e.copy(sb.tok[:], ps.tokp[:]))

            def fsc(e, _qf=qf):
                for pr8 in range(8):
                    e.tensor_mul(sb.q5[:, pr8 * 128:(pr8 + 1) * 128],
                                 _qf[:, pr8 * 128:(pr8 + 1) * 128], sb.gq5b[:])
                e.tensor_scalar_mul(sb.kw[:], sb.tok[:, 256:512], c.gcols[:, 2:3])
                return e.tensor_scalar_mul(sb.qw[:], sb.tok[:, 0:256], c.gcols[:, 2:3])
            p.step("vector", fsc)
            head = (p.sq, p.sq.n)

            stA, stB, stC, stD = [], [], [], []
            svS, svV = [], []
            for pr in range(8):
                g = gpair + pr
                st = g % NSET
                aq, fdq = ps.aq[st], ps.fdq[st]
                aw, bm, fm, dm, kcn, obt = (sb.aw[st], sb.bm[st], sb.fm[st],
                                            sb.dm[st], sb.kcn[st], sb.obt[st])
                s_, r = pr // 4, (pr % 4) * 32
                qff = qf[:, pr * 128:(pr + 1) * 128]
                kff = qf[:, (8 + pr) * 128:(9 + pr) * 128]
                q5f = sb.q5[:, pr * 128:(pr + 1) * 128]
                ktk = sb.tok[:, 256 + s_ * 128 + r: 256 + s_ * 128 + r + 32]
                vtk = sb.tok[:, 512 + s_ * 128 + r: 512 + s_ * 128 + r + 32]
                kwk = sb.kw[:, s_ * 128 + r: s_ * 128 + r + 32]
                qwk = sb.qw[:, s_ * 128 + r: s_ * 128 + r + 32]
                Ssl = sb.scgb[:, pr * 32:(pr + 1) * 32]
                Csl = sb.scgb[:, 256 + pr * 32: 256 + (pr + 1) * 32]
                Gsl = sb.scgb[:, 512 + pr * 32: 512 + (pr + 1) * 32]

                def f1(e, _q=qff, _k=kff, _S=Ssl, _C=Csl, _aq=aq, _g=g,
                       _head=head):
                    e.wait_ge(_head[0].h, _head[1])
                    if _g - (NSET - 1) > 0:
                        e.wait_ge(ss.v10.h, _g - (NSET - 1))
                    e.matmul(_aq[:, 0:128], _q, _k, start=True, stop=True)
                    e.matmul(_aq[:, 128:256], _k, _q, start=True, stop=True)
                    e.matmul(_aq[0:32, 256:384], _S, _q, start=True, stop=True)
                    e.matmul(_aq[0:32, 384:416], _S, _C, start=True, stop=True)
                    return e.matmul(_aq[:, 416:448], _k, _C,
                                    start=True, stop=True).then_inc(ss.t1.h, 1)
                stA.append(f1)

                def f2(e, _aq=aq, _aw=aw, _g=g):
                    e.wait_ge(ss.t1.h, _g + 1)
                    return e.copy(_aw[:], _aq[:, 0:448]).then_inc(ss.s2.h, 1)
                svS.append(f2)

                def f3(e, _aw=aw, _bm=bm, _g=g):
                    e.wait_ge(ss.s2.h, _g + 1)
                    return e.tensor_mul(_bm[:], _aw[:, 0:256],
                                        c.maskp[:]).then_inc(ss.v3.h, 1)
                v3 = f3

                def f4(e, _q=qff, _aw=aw, _fdq=fdq, _g=g):
                    e.wait_ge(ss.v3.h, _g + 1)
                    return e.matmul(_fdq[:, 0:128], _q, _aw[0:32, 256:384],
                                    start=True, stop=True).then_inc(ss.t4.h, 1)
                stB.append(f4)

                def f5(e, _fdq=fdq, _fm=fm, _g=g):
                    e.wait_ge(ss.t4.h, _g + 1)
                    return e.tensor_mul(_fm[:], _fdq[:, 0:128],
                                        c.maskp[:, 128:256]).then_inc(ss.v5.h, 1)
                v5 = f5

                def f6(e, _aw=aw, _bm=bm, _v=vtk, _aq=aq, _fdq=fdq, _g=g):
                    e.wait_ge(ss.v5.h, _g + 1)
                    e.matmul(_fdq[:, 128:256], _aw[:, 128:256], _bm[:, 128:256],
                             start=True, stop=True)
                    return e.matmul(_aq[:, 448:480], _bm[:, 0:128], _v,
                                    start=True, stop=True).then_inc(ss.t6.h, 1)
                stC.append(f6)

                def f7(e, _fdq=fdq, _dm=dm, _aw=aw, _kcn=kcn, _aq=aq, _g=g):
                    e.wait_ge(ss.t6.h, _g + 1)
                    e.tensor_mul(_dm[:], _fdq[:, 128:256], c.maskp[:, 128:256])
                    return e.scalar_tensor_tensor(
                        _kcn[:], _aw[:, 416:448], c.gcols[:, 1:2],
                        _aq[:, 448:480], ALU.mult, ALU.subtract
                    ).then_inc(ss.v7.h, 1)
                v7 = f7

                def f8(e, _q5=q5f, _q=qff, _G=Gsl, _v=vtk, _aw=aw, _bm=bm,
                       _fm=fm, _dm=dm, _kcn=kcn, _fdq=fdq, _kw=kwk, _qw=qwk,
                       _k=ktk, _g=g):
                    e.wait_ge(ss.v7.h, _g + 1)
                    e.matmul(_fdq[:, 256:288], _q5, _aw[0:32, 384:416],
                             start=True, stop=False)
                    e.matmul(_fdq[:, 256:288], _q, _G, start=False, stop=False)
                    e.matmul(_fdq[:, 256:288], _fm[:], _v, start=False, stop=False)
                    e.matmul(_fdq[:, 256:288], _bm[:, 128:256], _aw[:, 416:448],
                             start=False, stop=True)
                    e.matmul(_fdq[:, 288:320], _dm[:], _v, start=True, stop=False)
                    e.matmul(_fdq[:, 288:320], _bm[:, 128:256], _kcn[:],
                             start=False, stop=True)
                    e.matmul(_fdq[0:32, 320:352], _kw, _k, start=True, stop=True)
                    e.matmul(_fdq[0:32, 352:384], _qw, _v, start=True, stop=True)
                    return e.matmul(_fdq[0:32, 384:416], _kw, _kcn[:],
                                    start=True, stop=True).then_inc(ss.t8.h, 1)
                stD.append(f8)

                def f10(e, _fdq=fdq, _obt=obt, _pr=pr, _g=g):
                    e.wait_ge(ss.t8.h, _g + 1)
                    e.tensor_scalar_mul(_obt[:], _fdq[:, 256:288], c.gcols[:, 0:1])
                    e.drain()
                    e.tensor_add(sb.osb[:, _pr * 32:(_pr + 1) * 32],
                                 _obt[:], _fdq[:, 288:320])
                    Sf = sb.scg[:, _pr * 32:(_pr + 1) * 32]
                    Cf = sb.scg[:, 256 + _pr * 32: 256 + (_pr + 1) * 32]
                    Gf = sb.scg[:, 512 + _pr * 32: 512 + (_pr + 1) * 32]
                    e.scalar_tensor_tensor(Sf, Sf, GL, _fdq[0:32, 320:352],
                                           ALU.mult, ALU.add)
                    e.scalar_tensor_tensor(Cf, Cf, GL, _fdq[0:32, 352:384],
                                           ALU.mult, ALU.add)
                    return e.scalar_tensor_tensor(
                        Gf, Gf, GL, _fdq[0:32, 384:416],
                        ALU.mult, ALU.add).then_inc(ss.v10.h, 1)
                svV.append((v3, v5, v7, f10))

            ss.t1.n += 8
            ss.s2.n += 8
            ss.v3.n += 8
            ss.t4.n += 8
            ss.v5.n += 8
            ss.t6.n += 8
            ss.v7.n += 8
            ss.t8.n += 8
            ss.v10.n += 8
            gpair += 8

            for s in range(8 + 3):
                if 3 <= s and s - 3 < 8:
                    p.raw("tensor", stD[s - 3])
                    p.raw("vector", svV[s - 3][3])
                if 2 <= s and s - 2 < 8:
                    p.raw("tensor", stC[s - 2])
                    p.raw("vector", svV[s - 2][2])
                if 1 <= s and s - 1 < 8:
                    p.raw("tensor", stB[s - 1])
                    p.raw("vector", svV[s - 1][1])
                if s < 8:
                    p.raw("tensor", stA[s])
                    p.raw("scalar", svS[s])
                    p.raw("vector", svV[s][0])

            def fot(e, _v10=ss.v10.n):
                e.wait_ge(ss.v10.h, _v10)
                e.transpose(ps.ofmp[:, 0:128], sb.osb[:, 0:128], c.id_b)
                return e.transpose(ps.ofmp[:, 128:256], sb.osb[:, 128:256], c.id_b)
            p.step("tensor", fot)
            p.step("scalar", lambda e: e.copy(sb.ofm[:], ps.ofmp[:]))

            def fow(e, _ch=ch):
                io.o_out(e, _ch, sb.ofm, sOUT)
            p.issue("gpsimd", fow)
            sOUT.n += 16 * io.o_ndma

        def fdr(e, _n=sOUT.n):
            if _n:
                e.wait_ge(sOUT.h, _n)
            return e.nop()
        p.step("gpsimd", fdr)
    return sOUT


def build_p0_program():
    nc = make_nc()
    c = Ctx()
    decl_common(nc, c)
    io = Ctx()
    io.x = nc.dram_tensor("x", [TOKC, D], F32, kind="ExternalInput").ap()
    io.x1 = nc.dram_tensor("x1", [128, NT * 2048], F32, kind="ExternalOutput").ap()
    io.lnsc = nc.dram_tensor("lnsc", [128, NT * 2048], BF16, kind="Internal").ap()
    rb_scr = nc.dram_tensor("rb_scr", [NT, NE, 128, 256], BF16, kind="Internal").ap()
    io.rbsc = [rb_scr[t].transpose([1, 0, 2]) for t in range(NT)]
    io.rbsc_sl = lambda t, e0: rb_scr[t, e0:e0 + EPP].transpose([1, 0, 2])
    io.qkvd = nc.dram_tensor("qkv", [3, DFF, TOKC], BF16, kind="ExternalOutput").ap()
    wc_g = nc.dram_tensor("wc_g", [NE + 1, 128, WIMG], BF16, kind="ExternalOutput").ap()
    wc_u = nc.dram_tensor("wc_u", [NE + 1, 128, WIMG], BF16, kind="ExternalOutput").ap()
    wc_d = nc.dram_tensor("wc_d", [NE + 1, 128, WIMGD], BF16, kind="ExternalOutput").ap()
    io.wcache = None
    io.wcache_w = ([wc_g[i] for i in range(NE + 1)],
                   [wc_u[i] for i in range(NE + 1)],
                   [wc_d[i] for i in range(NE + 1)])
    io.cast_weights = True

    def qkv_out(e, t, sem, _io=io):
        for j, src in enumerate([io._fb.t1, io._fb.t2, io._fb.hh[0]]):
            e.dma_start(
                d3(_io.qkvd[j, :, t * TT:(t + 1) * TT]),
                s3(src, KF)).then_inc(sem.h, 16)
    io.qkv_out = qkv_out
    io.qkv_ndma = 3

    p = Prog(nc)
    sC = Sem(nc, "sC")
    io.sC = sC
    load_consts(p, c, sC)
    with contextlib.ExitStack() as stk:
        fb = FfnBufs(nc, stk)
        io._fb = fb
        emit_ffn_phase(p, c, fb, io, "p0")
        p.emit("p0")
    return nc


def build_scan_program():
    nc = make_nc()
    c = Ctx()
    decl_common(nc, c)
    io = Ctx()
    io.qkv_me = nc.dram_tensor("qkv_me", [NCH, 768, L], BF16, kind="ExternalInput").ap()
    io.o_me = nc.dram_tensor("o_me", [NCH, 2 * L, L], BF16, kind="ExternalOutput").ap()

    def qkv_in(e, ch, qc, qf, sem, _io=io):
        e.dma_start(s3(qc, 6), d3(_io.qkv_me[ch])).then_inc(sem.h, 16)
        e.dma_start(s3(qf, 24), d3(_io.qkv_me[ch], p=32)).then_inc(sem.h, 16)
    io.qkv_in = qkv_in

    def o_out(e, ch, ofm, sem, _io=io):
        e.dma_start(_io.o_me[ch, 0:128, :], ofm[:, 0:128]).then_inc(sem.h, 16)
        e.dma_start(_io.o_me[ch, 128:256, :], ofm[:, 128:256]).then_inc(sem.h, 16)
    io.o_out = o_out
    io.o_ndma = 2

    p = Prog(nc)
    sC = Sem(nc, "sC")
    io.sC = sC
    load_consts(p, c, sC)
    emit_scan(p, c, io)
    p.emit("scan")
    return nc


def build_p2_program():
    nc = make_nc()
    c = Ctx()
    decl_common(nc, c)
    io = Ctx()
    io.x1 = nc.dram_tensor("x1", [128, NT * 2048], F32, kind="ExternalInput").ap()
    io.o_mine = nc.dram_tensor("o_mine", [DFF, TOKC], BF16, kind="ExternalInput").ap()
    io.y = nc.dram_tensor("y", [TOKC, D], F32, kind="ExternalOutput").ap()
    io.lnsc = nc.dram_tensor("lnsc", [128, NT * 2048], BF16, kind="Internal").ap()
    rb_scr = nc.dram_tensor("rb_scr", [NT, NE, 128, 256], BF16, kind="Internal").ap()
    io.rbsc = [rb_scr[t].transpose([1, 0, 2]) for t in range(NT)]
    io.rbsc_sl = lambda t, e0: rb_scr[t, e0:e0 + EPP].transpose([1, 0, 2])
    wc_g = nc.dram_tensor("wc_g", [NE + 1, 128, WIMG], BF16, kind="ExternalInput").ap()
    wc_u = nc.dram_tensor("wc_u", [NE + 1, 128, WIMG], BF16, kind="ExternalInput").ap()
    wc_d = nc.dram_tensor("wc_d", [NE + 1, 128, WIMGD], BF16, kind="ExternalInput").ap()
    io.wcache = ([wc_g[i] for i in range(NE + 1)],
                 [wc_u[i] for i in range(NE + 1)],
                 [wc_d[i] for i in range(NE + 1)])
    io.cast_weights = False

    def o_load(e, t, _io=io):
        return e.dma_start(
            s3(_io._fb.o_in, KF),
            d3(_io.o_mine[:, t * TT:(t + 1) * TT]))
    io.o_load = o_load

    p = Prog(nc)
    sC = Sem(nc, "sC")
    io.sC = sC
    load_consts(p, c, sC)
    with contextlib.ExitStack() as stk:
        fb = FfnBufs(nc, stk)
        io._fb = fb
        emit_ffn_phase(p, c, fb, io, "p2")
        p.emit("p2")
    return nc


def kernel(L0_attn_x, L0_attn_Wq, L0_attn_Wk, L0_attn_Wv, L0_attn_Wo,
           L0_ffn_gate_w, L0_ffn_w_gate, L0_ffn_w_up, L0_ffn_w_down,
           L0_ffn_ws_gate, L0_ffn_ws_up, L0_ffn_ws_down):
    x = np.ascontiguousarray(np.asarray(L0_attn_x, np.float32))
    consts = host_consts()
    wmap = dict(
        wq=np.ascontiguousarray(L0_attn_Wq, np.float32),
        wk=np.ascontiguousarray(L0_attn_Wk, np.float32),
        wv=np.ascontiguousarray(L0_attn_Wv, np.float32),
        wo=np.ascontiguousarray(L0_attn_Wo, np.float32),
        gate_wT=np.ascontiguousarray(np.asarray(L0_ffn_gate_w, np.float32).T),
        we_gate=np.ascontiguousarray(L0_ffn_w_gate, np.float32),
        we_up=np.ascontiguousarray(L0_ffn_w_up, np.float32),
        we_down=np.ascontiguousarray(L0_ffn_w_down, np.float32),
        ws_gate=np.ascontiguousarray(L0_ffn_ws_gate, np.float32),
        ws_up=np.ascontiguousarray(L0_ffn_ws_up, np.float32),
        ws_down=np.ascontiguousarray(L0_ffn_ws_down, np.float32),
        **consts,
    )

    if FUSED:
        return kernel_fused(wmap, x)

    nc1 = build_p0_program()
    in1 = []
    for cid in range(NCORE):
        b, h = cid // 2, cid % 2
        m = dict(wmap)
        m["x"] = np.ascontiguousarray(x[b, h * TOKC:(h + 1) * TOKC, :])
        in1.append(m)
    r1 = run_bass_kernel_spmd(nc1, in1, core_ids=list(range(NCORE)))
    x1 = [r1.results[cid]["x1"] for cid in range(NCORE)]
    qkv = [r1.results[cid]["qkv"] for cid in range(NCORE)]
    wcg = r1.results[0]["wc_g"]
    wcu = r1.results[0]["wc_u"]
    wcd = r1.results[0]["wc_d"]

    nc2 = build_scan_program()
    in2 = []
    for cid in range(NCORE):
        b, hh = cid // 2, cid % 2
        qa, qb = qkv[2 * b], qkv[2 * b + 1]
        rows = slice(hh * 256, (hh + 1) * 256)
        me = np.empty((NCH, 768, L), qa.dtype)
        for j in range(3):
            full = np.concatenate([qa[j][rows], qb[j][rows]], axis=1)
            me[:, j * 256:(j + 1) * 256, :] = (
                full.reshape(256, NCH, L).transpose(1, 0, 2))
        in2.append({k: wmap[k] for k in wmap} | {"qkv_me": me})
    r2 = run_bass_kernel_spmd(nc2, in2, core_ids=list(range(NCORE)))

    nc3 = build_p2_program()
    in3 = []
    for cid in range(NCORE):
        b, h = cid // 2, cid % 2
        o0 = r2.results[2 * b]["o_me"]
        o1 = r2.results[2 * b + 1]["o_me"]
        ofull = np.concatenate([
            o0.transpose(1, 0, 2).reshape(256, T),
            o1.transpose(1, 0, 2).reshape(256, T)], axis=0)
        m = dict(wmap)
        m["x1"] = x1[cid]
        m["o_mine"] = np.ascontiguousarray(ofull[:, h * TOKC:(h + 1) * TOKC])
        m["wc_g"], m["wc_u"], m["wc_d"] = wcg, wcu, wcd
        in3.append(m)
    r3 = run_bass_kernel_spmd(nc3, in3, core_ids=list(range(NCORE)))

    y = np.empty((B, T, D), np.float32)
    for cid in range(NCORE):
        b, h = cid // 2, cid % 2
        y[b, h * TOKC:(h + 1) * TOKC, :] = r3.results[cid]["y"]
    return y


class DualInc:
    def __init__(self, *insts):
        self.insts = insts

    def then_inc(self, sem, val):
        for i in self.insts:
            i.then_inc(sem, val)
        return self


def d3dyn(x3, p=128):
    return x3.rearrange("b (a p) t -> b a p t", p=p).transpose([0, 2, 1, 3])


def build_fused_program():
    nc = make_nc()
    c = Ctx()
    decl_common(nc, c)

    x_in = nc.dram_tensor("x", [TOKC, D], F32, kind="ExternalInput").ap()
    y_out = nc.dram_tensor("y", [TOKC, D], F32, kind="ExternalOutput").ap()
    x1_scr = nc.dram_tensor("x1_scr", [128, NT * 2048], F32, kind="Internal").ap()
    lnsc = nc.dram_tensor("lnsc_s", [128, NT * 2048], BF16, kind="Internal").ap()
    rb_scr = nc.dram_tensor("rb_scr_s", [NT, NE, 128, 256], BF16, kind="Internal").ap()
    rbsc = [rb_scr[t].transpose([1, 0, 2]) for t in range(NT)]
    rbsc_sl = lambda t, e0: rb_scr[t, e0:e0 + EPP].transpose([1, 0, 2])
    wc_g = nc.dram_tensor("wc_g", [NE + 1, 128, WIMG], BF16, kind="Internal").ap()
    wc_u = nc.dram_tensor("wc_u", [NE + 1, 128, WIMG], BF16, kind="Internal").ap()
    wc_d = nc.dram_tensor("wc_d", [NE + 1, 128, WIMGD], BF16, kind="Internal").ap()
    wcache = ([wc_g[i] for i in range(NE + 1)],
              [wc_u[i] for i in range(NE + 1)],
              [wc_d[i] for i in range(NE + 1)])
    shm_qkv = nc.dram_tensor("shm_qkv", [2, NCH, 768, L], BF16,
                             kind="Internal", addr_space="Shared").ap()
    shm_o = nc.dram_tensor("shm_o", [DFF, T], BF16,
                           kind="Internal", addr_space="Shared").ap()

    p = Prog(nc)
    sC = Sem(nc, "sC")

    io0 = Ctx()
    io0.x = x_in
    io0.x1 = x1_scr
    io0.lnsc = lnsc
    io0.rbsc, io0.rbsc_sl = rbsc, rbsc_sl
    io0.wcache = None
    io0.wcache_w = wcache
    io0.cast_weights = True
    io0.sC = sC

    _hh_regs = {}

    def hh_reg(e):
        if id(e) not in _hh_regs:
            _hh_regs[id(e)] = e.to_reg(e.partition_id() % 2)
        return _hh_regs[id(e)]

    def qkv_out(e, t, sem, _fb=None):
        fb = io0._fb

        def arm(hh):
            for j, srcb in enumerate([fb.t1, fb.t2, fb.hh[0]]):
                for m in range(KF):
                    for ch in range(2):
                        e.dma_start(
                            shm_qkv[m // 2, hh * 32 + 2 * t + ch,
                                    j * 256 + (m % 2) * 128: j * 256 + (m % 2) * 128 + 128, :],
                            srcb[:, m * 256 + ch * 128: m * 256 + ch * 128 + 128]
                        ).then_inc(sem.h, 16)
        r = hh_reg(e)
        with e.If_eq(r, 0):
            arm(0)
        with e.Else():
            arm(1)
    io0.qkv_out = qkv_out
    io0.qkv_ndma = 24

    load_consts(p, c, sC)
    with contextlib.ExitStack() as stk:
        fb = FfnBufs(nc, stk)
        io0._fb = fb
        emit_ffn_phase(p, c, fb, io0, "p0")
        p.emit("p0")
    nc.all_core_barrier()

    ios = Ctx()
    ios.sC = sC

    def qkv_in(e, ch, qc, qf, sem):
        r = hh_reg(e)

        def arm(hh):
            blk = shm_qkv[hh, ch]
            e.dma_start(s3(qc, 6), d3(blk)).then_inc(sem.h, 16)
            e.dma_start(s3(qf, 24), d3(blk, p=32)).then_inc(sem.h, 16)
        with e.If_eq(r, 0):
            arm(0)
        with e.Else():
            arm(1)
    ios.qkv_in = qkv_in

    def o_out(e, ch, ofm, sem):
        r = hh_reg(e)

        def arm(hh):
            for s in range(2):
                e.dma_start(
                    shm_o[hh * 256 + s * 128: hh * 256 + (s + 1) * 128,
                          ch * L:(ch + 1) * L],
                    ofm[:, s * 128:(s + 1) * 128]).then_inc(sem.h, 16)
        with e.If_eq(r, 0):
            arm(0)
        with e.Else():
            arm(1)
    ios.o_out = o_out
    ios.o_ndma = 2

    emit_scan(p, c, ios)
    p.emit("scan")
    nc.all_core_barrier()

    io2 = Ctx()
    io2.x1 = x1_scr
    io2.lnsc = lnsc
    io2.rbsc, io2.rbsc_sl = rbsc, rbsc_sl
    io2.y = y_out
    io2.wcache = wcache
    io2.cast_weights = False
    io2.sC = sC

    def o_load(e, t):
        fb2 = io2._fb
        r = hh_reg(e)
        with e.If_eq(r, 0):
            i1 = e.dma_start(s3(fb2.o_in, KF),
                             d3(shm_o[:, t * TT:(t + 1) * TT]))
        with e.Else():
            i2 = e.dma_start(s3(fb2.o_in, KF),
                             d3(shm_o[:, TOKC + t * TT: TOKC + (t + 1) * TT]))
        return DualInc(i1, i2)
    io2.o_load = o_load

    with contextlib.ExitStack() as stk:
        fb2 = FfnBufs(nc, stk)
        io2._fb = fb2
        emit_ffn_phase(p, c, fb2, io2, "p2")
        p.emit("p2")
    return nc


FUSED = True


def kernel_fused(wmap, x):
    nc = build_fused_program()
    ins = []
    for cid in range(NCORE):
        b, h = cid // 2, cid % 2
        m = dict(wmap)
        m["x"] = np.ascontiguousarray(x[b, h * TOKC:(h + 1) * TOKC, :])
        ins.append(m)
    res = run_bass_kernel_spmd(nc, ins, core_ids=list(range(NCORE)))
    y = np.empty((B, T, D), np.float32)
    for cid in range(NCORE):
        b, h = cid // 2, cid % 2
        y[b, h * TOKC:(h + 1) * TOKC, :] = res.results[cid]["y"]
    return y


# version 18
# speedup vs baseline: 1.0405x; 1.0108x over previous
import contextlib
import numpy as np
import concourse.bass as bass
import concourse.mybir as mybir
from concourse.bass_utils import run_bass_kernel_spmd

F32 = mybir.dt.float32
BF16 = mybir.dt.bfloat16
AF = mybir.ActivationFunctionType
ALU = mybir.AluOpType
ds = bass.ds

B, T, D = 4, 8192, 1024
NCORE = 8
TOKC = T // 2
TT = 256
NT = TOKC // TT
DFF = 512
NE = 16
NPASS = 4
EPP = NE // NPASS
GAMMA = 0.99
H, DK, DV = 16, 32, 32
HPC = 8
L = 128
NCH = T // L
GL = float(GAMMA ** L)
EPS = 1e-5
KD = D // 128
KF = DFF // 128
WIMG = KD * DFF
WIMGD = KF * D


def host_consts():
    ii = np.arange(L, dtype=np.float64)
    mg = np.where(ii[:, None] <= ii[None, :], GAMMA ** (ii[None, :] - ii[:, None]), 0.0)
    m5 = np.where(ii[:, None] < ii[None, :], GAMMA ** (ii[None, :] - ii[:, None]), 0.0)
    maskpack = np.concatenate([m5, mg], axis=1)
    gq5 = np.broadcast_to(GAMMA ** (ii + 1), (128, L)).copy()
    gcols = np.zeros((128, 4))
    gcols[:, 0] = GAMMA ** (ii + 1)
    gcols[:, 1] = -(GAMMA ** (ii + 1))
    gcols[:, 2] = GAMMA ** (L - 1 - ii)
    gcols[:, 3] = EPS
    sel = np.zeros((16, 16 * 128))
    for e in range(16):
        sel[e, e * 128:(e + 1) * 128] = 1.0
    return {
        "c_id128": np.eye(128, dtype=np.float32),
        "c_ones_k": np.ones((128, 1), np.float32),
        "c_ones_r": np.ones((1, 128), np.float32),
        "c_sel": sel.astype(np.float32),
        "c_maskpack": maskpack.astype(np.float32),
        "c_gq5": gq5.astype(np.float32),
        "c_gcols": gcols.astype(np.float32),
    }


CONST_SHAPES = {
    "c_id128": [128, 128], "c_ones_k": [128, 1], "c_ones_r": [1, 128],
    "c_sel": [16, 2048], "c_maskpack": [128, 256], "c_gq5": [128, 128],
    "c_gcols": [128, 4],
}


DEBUG = False

_UNIQ = [0]


def uniq(name):
    _UNIQ[0] += 1
    return f"{name}_{_UNIQ[0]}"


class Sem:
    def __init__(self, nc, name):
        self.h = nc.alloc_semaphore(uniq(name))
        self.n = 0


class Prog:
    ENGS = ("sync", "gpsimd", "scalar", "vector", "tensor")

    def __init__(self, nc):
        self.nc = nc
        self.th = {e: [] for e in self.ENGS}
        self._nseq = 0
        self.sq = Sem(nc, "sq0")
        self.prev_chain = None
        self._qs = {}
        self._pending = []

    def qsem(self, eng):
        if eng not in self._qs:
            self._qs[eng] = Sem(self.nc, f"q_{eng}")
        return self._qs[eng]

    def new_chain(self):
        self.prev_chain = (self.sq, self.sq.n)
        self._nseq += 1
        self.sq = Sem(self.nc, f"sq{self._nseq}")

    def step(self, eng, fn, dma=False, extra_waits=()):
        ew = list(extra_waits) + self._pending
        self._pending = []
        tgt, sem = self.sq.n, self.sq
        if dma:
            qs = self.qsem(eng)
            qs.n += 16
            qtgt = qs.n

            def th(e, _fn=fn, _t=tgt, _s=sem, _qs=qs, _ew=tuple(ew)):
                for (s2, v2) in _ew:
                    if v2 > 0:
                        e.wait_ge(s2.h, v2)
                if _t > 0:
                    e.wait_ge(_s.h, _t)
                _fn(e).then_inc(_qs.h, 16)
            self.th[eng].append(th)
            self._pending.append((qs, qtgt))
        else:
            def th(e, _fn=fn, _t=tgt, _s=sem, _ew=tuple(ew)):
                for (s2, v2) in _ew:
                    if v2 > 0:
                        e.wait_ge(s2.h, v2)
                if _t > 0:
                    e.wait_ge(_s.h, _t)
                _fn(e).then_inc(_s.h, 1)
            self.th[eng].append(th)
            self.sq.n += 1

    def raw(self, eng, fn):
        self.th[eng].append(fn)

    def issue(self, eng, fn, extra_waits=(), chain_wait=True):
        tgt, sem, ew = self.sq.n, self.sq, tuple(extra_waits)

        def th(e, _fn=fn, _t=tgt if chain_wait else 0, _s=sem, _ew=ew):
            for (s2, v2) in _ew:
                if v2 > 0:
                    e.wait_ge(s2.h, v2)
            if _t > 0:
                e.wait_ge(_s.h, _t)
            _fn(e)
        self.th[eng].append(th)

    def emit(self, name="blk"):
        nc, th = self.nc, self.th
        with nc.Block(name) as block:
            @block.sync
            def _(e):
                for f in th["sync"]:
                    f(e)

            @block.gpsimd
            def _(e):
                for f in th["gpsimd"]:
                    f(e)

            @block.scalar
            def _(e):
                for f in th["scalar"]:
                    f(e)

            @block.vector
            def _(e):
                for f in th["vector"]:
                    f(e)

            @block.tensor
            def _(e):
                for f in th["tensor"]:
                    f(e)
        self.th = {e: [] for e in self.ENGS}


class Ctx:
    pass


def make_nc():
    return bass.Bass("TRN2", target_bir_lowering=False, num_devices=NCORE)


def d3(x2d, p=128):
    return x2d.rearrange("(a p) n -> a p n", p=p).transpose([1, 0, 2])


def s3(sb2d, a):
    return sb2d.rearrange("p (a n) -> p a n", a=a)


def w_ap(w):
    return d3(w)


def w3_ap(w, e):
    return d3(w[e])


def decl_common(nc, c):
    c.wq = nc.dram_tensor("wq", [D, DFF], F32, kind="ExternalInput").ap()
    c.wk = nc.dram_tensor("wk", [D, DFF], F32, kind="ExternalInput").ap()
    c.wv = nc.dram_tensor("wv", [D, DFF], F32, kind="ExternalInput").ap()
    c.wo = nc.dram_tensor("wo", [DFF, D], F32, kind="ExternalInput").ap()
    c.gate_wT = nc.dram_tensor("gate_wT", [D, NE], F32, kind="ExternalInput").ap()
    c.we_gate = nc.dram_tensor("we_gate", [NE, D, DFF], F32, kind="ExternalInput").ap()
    c.we_up = nc.dram_tensor("we_up", [NE, D, DFF], F32, kind="ExternalInput").ap()
    c.we_down = nc.dram_tensor("we_down", [NE, DFF, D], F32, kind="ExternalInput").ap()
    c.ws_gate = nc.dram_tensor("ws_gate", [D, DFF], F32, kind="ExternalInput").ap()
    c.ws_up = nc.dram_tensor("ws_up", [D, DFF], F32, kind="ExternalInput").ap()
    c.ws_down = nc.dram_tensor("ws_down", [DFF, D], F32, kind="ExternalInput").ap()
    for n, shp in CONST_SHAPES.items():
        setattr(c, n, nc.dram_tensor(n, shp, F32, kind="ExternalInput").ap())
    c.id_f = nc.alloc_sbuf_tensor("id_f", [128, 128], F32).ap()
    c.id_b = nc.alloc_sbuf_tensor("id_b", [128, 128], BF16).ap()
    c.ones_k = nc.alloc_sbuf_tensor("ones_k", [128, 1], BF16).ap()
    c.ones_r = nc.alloc_sbuf_tensor("ones_r", [1, 128], F32).ap()
    c.sel_b = nc.alloc_sbuf_tensor("sel_b", [16, 2048], BF16).ap()
    c.maskp = nc.alloc_sbuf_tensor("maskp", [128, 256], F32).ap()
    c.gq5 = nc.alloc_sbuf_tensor("gq5", [128, 128], F32).ap()
    c.gcols = nc.alloc_sbuf_tensor("gcols", [128, 4], F32).ap()
    c.gatewT = nc.alloc_sbuf_tensor("gatewT", [128, KD * NE], F32).ap()


def load_consts(p, c, sC):
    def f(e):
        e.dma_start(c.id_f, c.c_id128).then_inc(sC.h, 16)
        e.dma_start(c.id_b, c.c_id128).then_inc(sC.h, 16)
        e.dma_start(c.ones_k, c.c_ones_k).then_inc(sC.h, 16)
        e.dma_start(c.ones_r, c.c_ones_r).then_inc(sC.h, 16)
        e.dma_start(c.sel_b, c.c_sel).then_inc(sC.h, 16)
        e.dma_start(c.maskp, c.c_maskpack).then_inc(sC.h, 16)
        e.dma_start(c.gq5, c.c_gq5).then_inc(sC.h, 16)
        e.dma_start(c.gcols, c.c_gcols).then_inc(sC.h, 16)
        e.dma_start(s3(c.gatewT, KD), d3(c.gate_wT)).then_inc(sC.h, 16)
    p.th["gpsimd"].append(f)
    sC.n += 9 * 16


class FfnBufs:
    def __init__(self, nc, stk):
        A = lambda n, s, d: stk.enter_context(nc.sbuf_tensor(uniq(n), s, d)).ap()
        self.wg = [A(f"wg{i}", [128, WIMG], BF16) for i in range(5)]
        self.wu = [A(f"wu{i}", [128, WIMG], BF16) for i in range(5)]
        self.wd = [A(f"wd{i}", [128, WIMGD], BF16) for i in range(5)]
        self.x_in = [A("x_in0", [128, 2048], F32), A("x_in1", [128, 2048], F32)]
        self.x_fm = A("x_fm", [128, 2048], F32)
        self.ln_f = A("ln_f", [128, 2048], F32)
        self.ln_b = [A("ln_b0", [128, 2048], BF16), A("ln_b1", [128, 2048], BF16)]
        self.rbs = [A("rbs0", [128, 1280], BF16), A("rbs1", [128, 1280], BF16)]
        self.scr = A("scr", [128, 4096], BF16)
        self.hh = [A("hh0", [128, 1024], BF16), A("hh1", [128, 1024], BF16)]
        self.xout = [A(f"xout{i}", [128, 2048], F32) for i in range(2)]
        self.stat = A("stat", [1, 1024], F32)
        self.scores = A("scores", [128, 32], F32)
        self.maxes = A("maxes", [128, 16], F32)
        self.rsm = A("rsm", [128, 16], F32)
        self.istmp = A("istmp", [128, 32], F32)
        self.route = A("route", [128, 32], F32)
        self.rt4 = A("rt4", [16, 1024], BF16)
        self.x_bf = self.scr[:, 0:2048]
        self.sq_bf = self.scr[:, 2048:4096]
        self.t1 = self.scr[:, 0:1024]
        self.t2 = self.scr[:, 1024:2048]
        self.u_sb = self.scr[:, 2048:3072]
        self.o_in = self.wg[4][:, 0:1024]


class FfnPsum:
    def __init__(self, nc, stk):
        E = stk.enter_context
        self.g = E(nc.psum_tensor(uniq("pg"), [128, KF * TT], F32)).ap()
        self.u = E(nc.psum_tensor(uniq("pu"), [128, KF * TT], F32)).ap()
        self.yacc = E(nc.psum_tensor(uniq("pyacc"), [128, KD * TT], F32)).ap()
        self.misc = self.g[:, 0:512]
        self.rbst = self.u[:, 0:512]


def emit_ln(p, c, fb, pp, src, lnb):
    p.step("vector", lambda e: e.tensor_copy(fb.x_bf, src[:, 0:2048]))
    p.step("scalar", lambda e: e.activation(fb.sq_bf, src[:, 0:2048], AF.Square))

    def f(e):
        for k in range(KD):
            e.matmul(pp.misc[0:1, 0:256], c.ones_k,
                     fb.x_bf[:, k * 256:(k + 1) * 256],
                     start=(k == 0), stop=(k == KD - 1))
        for k in range(KD):
            last = e.matmul(pp.misc[0:1, 256:512], c.ones_k,
                            fb.sq_bf[:, k * 256:(k + 1) * 256],
                            start=(k == 0), stop=(k == KD - 1))
        return last
    p.step("tensor", f)

    def f(e):
        e.tensor_scalar_mul(fb.stat[:, 0:256], pp.misc[0:1, 0:256], 1.0 / D)
        e.tensor_scalar_mul(fb.stat[:, 256:512], pp.misc[0:1, 256:512], 1.0 / D)
        e.drain()
        e.tensor_mul(fb.stat[:, 512:768], fb.stat[:, 0:256], fb.stat[:, 0:256])
        e.drain()
        return e.tensor_sub(fb.stat[:, 512:768], fb.stat[:, 256:512],
                            fb.stat[:, 512:768])
    p.step("vector", f)
    p.step("scalar", lambda e: e.activation(fb.stat[:, 256:512], fb.stat[:, 512:768],
                                            AF.Sqrt, bias=c.gcols[0:1, 3:4]))

    def f(e):
        e.reciprocal(fb.stat[:, 512:768], fb.stat[:, 256:512])
        e.drain()
        return e.scalar_tensor_tensor(fb.stat[:, 768:1024], fb.stat[:, 0:256], -1.0,
                                      fb.stat[:, 512:768], ALU.mult, ALU.mult)
    p.step("vector", f)

    def f(e):
        e.matmul(pp.misc[:, 0:256], c.ones_r, fb.stat[:, 512:768],
                 start=True, stop=True)
        return e.matmul(pp.misc[:, 256:512], c.ones_r, fb.stat[:, 768:1024],
                        start=True, stop=True)
    p.step("tensor", f)

    def f(e):
        for k in range(KD):
            sl = slice(k * 256, (k + 1) * 256)
            e.tensor_mul(fb.ln_f[:, sl], src[:, sl], pp.misc[:, 0:256])
            e.drain()
            e.tensor_add(fb.ln_f[:, sl], fb.ln_f[:, sl], pp.misc[:, 256:512])
        e.drain()
        return e.tensor_copy(lnb[:], fb.ln_f[:])
    p.step("vector", f)


def emit_route(p, c, fb, pp):
    def f(e):
        for m in range(2):
            for k in range(KD):
                last = e.matmul(pp.misc[:, m * 16:(m + 1) * 16],
                                fb.ln_f[:, k * 256 + m * 128: k * 256 + m * 128 + 128],
                                c.gatewT[:, k * 16:(k + 1) * 16],
                                start=(k == 0), stop=(k == KD - 1))
        return last
    p.step("tensor", f)
    p.step("scalar", lambda e: e.activation(fb.scores[:], pp.misc[:, 0:32], AF.Sigmoid))

    def f(e):
        for m in range(2):
            sc = fb.scores[:, m * 16:(m + 1) * 16]
            mx = fb.maxes[:, m * 8:(m + 1) * 8]
            e.max(out=mx, in_=sc)
            e.drain()
            r = fb.rsm[:, m * 8:(m + 1) * 8]
            m1, m2 = mx[:, 0:1], mx[:, 1:2]
            den, rec, w1, w2 = r[:, 0:1], r[:, 1:2], r[:, 2:3], r[:, 3:4]
            e.tensor_scalar(den, m1, m2, 1e-20, ALU.add, ALU.add)
            e.drain()
            e.reciprocal(rec, den)
            e.drain()
            e.tensor_scalar(w1, m1, rec, 1.25, ALU.mult, ALU.mult)
            e.tensor_scalar(w2, m2, rec, 1.25, ALU.mult, ALU.mult)
            e.drain()
            rt = fb.route[:, m * 16:(m + 1) * 16]
            it = fb.istmp[:, m * 16:(m + 1) * 16]
            e.tensor_single_scalar(it, sc, m1, ALU.is_equal)
            e.drain()
            e.tensor_scalar_mul(rt, it, w1)
            e.drain()
            e.tensor_single_scalar(it, sc, m2, ALU.is_equal)
            e.drain()
            last = e.scalar_tensor_tensor(rt, it, w2, rt, ALU.mult, ALU.add)
            e.drain()
        return last
    p.step("vector", f)

    def f(e):
        e.transpose(pp.misc[0:16, 0:128], fb.route[:, 0:16], c.id_f)
        return e.transpose(pp.misc[0:16, 128:256], fb.route[:, 16:32], c.id_f)
    p.step("tensor", f)

    def f(e):
        for r in range(4):
            last = e.tensor_copy(fb.rt4[:, r * 256:(r + 1) * 256],
                                 pp.misc[0:16, 0:256])
        return last
    p.step("vector", f)


def emit_rb_all(p, c, fb, pp):
    for eid in range(NE):
        def fmm(e, _e=eid):
            return e.matmul(pp.rbst[:, 0:256] if _e % 2 == 0 else pp.rbst[:, 256:512],
                            c.sel_b[:, _e * 128:(_e + 1) * 128],
                            fb.rt4[:, 0:256], start=True, stop=True)
        p.step("tensor", fmm)

        def fcp(e, _e=eid):
            return e.copy(fb.scr[:, _e * 256:(_e + 1) * 256],
                          pp.rbst[:, 0:256] if _e % 2 == 0 else pp.rbst[:, 256:512])
        p.step("scalar", fcp)


class ESems:
    def __init__(self, nc, tag):
        self.gu = Sem(nc, f"eGU_{tag}")
        self.si = Sem(nc, f"eSI_{tag}")
        self.uc = Sem(nc, f"eUC_{tag}")
        self.gm = Sem(nc, f"eGM_{tag}")
        self.h = Sem(nc, f"eH_{tag}")
        self.dn = Sem(nc, f"eDN_{tag}")


def rb_bcast(rbs_par, sl):
    return rbs_par[:, sl * 256:(sl + 1) * 256] \
        .rearrange("p (o n) -> p o n", o=1).broadcast_to([128, KF, 256])


def emit_experts(p, c, fb, pp, slots, es, lnb, first_waits=(), dn_wait=None,
                 si_first_waits=(), h_first_waits=()):
    nslots = len(slots)
    chain0 = None
    pend = list(p._pending) + list(first_waits)
    p._pending = []

    def dn_of(i, wd, first, last_):
        def fd(e, _wd=wd, _first=first, _last=last_, _q=i % 2):
            for m in range(KD):
                for k in range(KF):
                    last = e.matmul(pp.yacc[:, m * TT:(m + 1) * TT],
                                    _wd[:, k * D + m * 128: k * D + m * 128 + 128],
                                    fb.hh[_q][:, k * TT:(k + 1) * TT],
                                    start=(_first and k == 0 and m % 2 == 0),
                                    stop=(_last and k == KF - 1),
                                    skip_group_check=True)
            return last
        return fd

    pend_dn = None
    for i, (wg, wu, wd, rb) in enumerate(slots):
        w_si, w_uc = es.si.n, es.uc.n
        cw = chain0 if i == 0 else None

        def fgu(e, _wg=wg, _wu=wu, _wsi=w_si, _wuc=w_uc, _cw=cw,
                _pend=tuple(pend) if i == 0 else ()):
            for (s2, v2) in _pend:
                if v2 > 0:
                    e.wait_ge(s2.h, v2)
            if _cw is not None and _cw[1] > 0:
                e.wait_ge(_cw[0].h, _cw[1])
            if _wsi > 0:
                e.wait_ge(es.si.h, _wsi)
            for m in range(KF):
                for k in range(KD):
                    lg = e.matmul(pp.g[:, m * TT:(m + 1) * TT],
                                  _wg[:, k * DFF + m * 128: k * DFF + m * 128 + 128],
                                  lnb[:, k * TT:(k + 1) * TT],
                                  start=(k == 0), stop=(k == KD - 1))
            lg.then_inc(es.gm.h, 1)
            if _wuc > 0:
                e.wait_ge(es.uc.h, _wuc)
            for m in range(KF):
                for k in range(KD):
                    last = e.matmul(pp.u[:, m * TT:(m + 1) * TT],
                                    _wu[:, k * DFF + m * 128: k * DFF + m * 128 + 128],
                                    lnb[:, k * TT:(k + 1) * TT],
                                    start=(k == 0), stop=(k == KD - 1))
            last.then_inc(es.gu.h, 1)
            return last
        p.raw("tensor", fgu)
        es.gu.n += 1
        es.gm.n += 1

        w_gu, w_h2 = es.gm.n, es.h.n

        def fsi(e, _wgu=w_gu, _wh=w_h2,
                _fw=tuple(si_first_waits) if i == 0 else ()):
            for (s2, v2) in _fw:
                if v2 > 0:
                    e.wait_ge(s2.h, v2)
            e.wait_ge(es.gm.h, _wgu)
            if _wh > 0:
                e.wait_ge(es.h.h, _wh)
            return e.activation(fb.t1, pp.g[:], AF.Silu).then_inc(es.si.h, 1)
        p.raw("scalar", fsi)
        es.si.n += 1

        w_gu2 = es.gu.n

        def fuc(e, _wgu=w_gu2,
                _fw=tuple(h_first_waits) if i == 0 else ()):
            for (s2, v2) in _fw:
                if v2 > 0:
                    e.wait_ge(s2.h, v2)
            e.wait_ge(es.gu.h, _wgu)
            return e.tensor_copy(fb.u_sb, pp.u[:]).then_inc(es.uc.h, 1)
        p.raw("vector", fuc)
        es.uc.n += 1

        w_si2, w_dn = es.si.n, es.dn.n

        def fh(e, _rb=rb, _wsi=w_si2, _wdn=w_dn, _q=i % 2):
            e.wait_ge(es.si.h, _wsi)
            if _wdn > 0:
                e.wait_ge(es.dn.h, _wdn)
            e.drain()
            e.tensor_mul(fb.t2, fb.t1, fb.u_sb)
            e.drain()
            if _rb is None:
                last = e.tensor_scalar_mul(fb.hh[_q][:], fb.t2, 0.5)
            else:
                last = e.tensor_mul(s3(fb.hh[_q], KF), s3(fb.t2, KF), _rb)
            last.then_inc(es.h.h, 1)
            return last
        p.raw("vector", fh)
        es.h.n += 1

        if pend_dn is not None:
            fd_prev, w_h_prev, dnw = pend_dn

            def fdp(e, _fd=fd_prev, _wh=w_h_prev, _dnw=dnw):
                if _dnw is not None and _dnw[1] > 0:
                    e.wait_ge(_dnw[0].h, _dnw[1])
                e.wait_ge(es.h.h, _wh)
                _fd(e).then_inc(es.dn.h, 1)
            p.raw("tensor", fdp)
            es.dn.n += 1
        pend_dn = (dn_of(i, wd, i == 0, i == nslots - 1), es.h.n,
                   dn_wait if i == 0 else None)

    fd_last, w_h_last, dnw = pend_dn

    def fdl(e, _fd=fd_last, _wh=w_h_last, _dnw=dnw):
        if _dnw is not None and _dnw[1] > 0:
            e.wait_ge(_dnw[0].h, _dnw[1])
        e.wait_ge(es.h.h, _wh)
        _fd(e).then_inc(es.dn.h, 1)
    p.raw("tensor", fdl)
    es.dn.n += 1


def emit_ffn_phase(p, c, fb, io, phase):
    nc = p.nc
    sW = Sem(nc, f"sW_{phase}")
    sOUTg = Sem(nc, f"sOUTg_{phase}")
    sOUTs = Sem(nc, f"sOUTs_{phase}")
    sINs = [Sem(nc, f"sINs{i}_{phase}") for i in range(2)]
    sINg = [Sem(nc, f"sINg{i}_{phase}") for i in range(2)]
    stk_ph = contextlib.ExitStack()
    pp = FfnPsum(nc, stk_ph)
    es = ESems(nc, phase)

    p.new_chain()
    pcL = p.prev_chain
    if phase == "p2":
        def fwo(e, _pc=pcL):
            if _pc is not None and _pc[1] > 0:
                e.wait_ge(_pc[0].h, _pc[1])
            e.dma_start(s3(fb.wd[4], KF), w_ap(c.wo)).then_inc(sW.h, 16)
        p.th["gpsimd"].append(fwo)
        sW.n += 16
    wo_tgt = sW.n

    if io.cast_weights:
        sWc = [Sem(nc, f"sWc{j}_p0") for j in range(3)]
        sWcw = [Sem(nc, f"sWcw{j}_p0") for j in range(3)]
        srcs3 = [lambda e_: w3_ap(c.we_gate, e_), lambda e_: w3_ap(c.we_up, e_),
                 lambda e_: w3_ap(c.we_down, e_)]
        stg = [fb.wg[4], fb.wu[4], fb.wd[4]]
        kds = [KD, KD, KF]

        def fcast_w(e, _pc=pcL):
            if _pc is not None and _pc[1] > 0:
                e.wait_ge(_pc[0].h, _pc[1])
            for eid in range(NE + 1):
                for j in range(3):
                    if eid >= 1:
                        e.wait_ge(sWcw[j].h, eid * 16)
                    if eid <= NE - 1:
                        e.dma_start(s3(stg[j], kds[j]), srcs3[j](eid)
                                    ).then_inc(sWc[j].h, 16)
                    else:
                        wsrc = [c.ws_gate, c.ws_up, c.ws_down][j]
                        e.dma_start(s3(stg[j], kds[j]), w_ap(wsrc)
                                    ).then_inc(sWc[j].h, 16)
        p.th["gpsimd"].append(fcast_w)

        def fwrite_w(e):
            for eid in range(NE + 1):
                for j in range(3):
                    e.wait_ge(sWc[j].h, (eid + 1) * 16)
                    e.dma_start(io.wcache_w[j][eid], stg[j][:]
                                ).then_inc(sWcw[j].h, 16)
        p.th["sync"].append(fwrite_w)
        for j in range(3):
            sWc[j].n = 16 * (NE + 1)
            sWcw[j].n = 16 * (NE + 1)
        cache_ready = 16 * (NE + 1)
    else:
        sWcw, cache_ready = None, 0

    for t in range(NT):
        ew = [(sOUTg, sOUTg.n), (sOUTs, sOUTs.n)]
        if t == 0:
            ew.append((io.sC, io.sC.n))
            ew.append((sW, wo_tgt))
        ewx = ew
        pc_extra = pcL if t == 0 else None

        if phase == "p0":
            def fx(e, _t=t, _pc=pc_extra, _ew=tuple(ewx)):
                for (s2, v2) in _ew:
                    if v2 > 0:
                        e.wait_ge(s2.h, v2)
                if _pc is not None and _pc[1] > 0:
                    e.wait_ge(_pc[0].h, _pc[1])
                return e.dma_start(s3(fb.x_in[0], 2),
                                   d3(io.x[_t * TT:(_t + 1) * TT, :]))
            p.step("sync", fx, dma=True)

            def ftr(e):
                for s in range(2):
                    for fc in range(KD):
                        last = e.transpose(
                            pp.yacc[:, (fc * 2 + s) * 128:(fc * 2 + s) * 128 + 128],
                            fb.x_in[0][:, s * 1024 + fc * 128: s * 1024 + fc * 128 + 128],
                            c.id_f)
                return last
            p.step("tensor", ftr)
            p.step("vector", lambda e: e.tensor_copy(fb.x_fm[:], pp.yacc[:]))
        else:
            def fx(e, _t=t, _pc=pc_extra, _ew=tuple(ewx)):
                for (s2, v2) in _ew:
                    if v2 > 0:
                        e.wait_ge(s2.h, v2)
                if _pc is not None and _pc[1] > 0:
                    e.wait_ge(_pc[0].h, _pc[1])
                return e.dma_start(fb.x_in[0][:],
                                   io.x1[:, _t * 2048:(_t + 1) * 2048])
            p.step("sync", fx, dma=True)

            def fo(e, _t=t):
                return io.o_load(e, _t)
            p.step("gpsimd", fo, dma=True)

            def fatt(e):
                for m in range(KD):
                    for k in range(KF):
                        last = e.matmul(
                            pp.yacc[:, m * TT:(m + 1) * TT],
                            fb.wd[4][:, k * D + m * 128: k * D + m * 128 + 128],
                            fb.o_in[:, k * TT:(k + 1) * TT],
                            start=(k == 0), stop=(k == KF - 1))
                return last
            p.step("tensor", fatt)
            p.step("vector", lambda e: e.tensor_add(fb.x_fm[:], fb.x_in[0][:],
                                                    pp.yacc[:]))

        def fwx(e, _t=t):
            e.dma_start(io.x1[:, _t * 2048:(_t + 1) * 2048],
                        fb.x_fm[:]).then_inc(sOUTs.h, 16)
        p.issue("sync", fwx)
        sOUTs.n += 16

        emit_ln(p, c, fb, pp, fb.x_fm, fb.ln_b[0])
        emit_route(p, c, fb, pp)

        def fsc(e, _t=t):
            e.dma_start(io.lnsc[:, _t * 2048:(_t + 1) * 2048],
                        fb.ln_b[0][:]).then_inc(sOUTg.h, 16)
        p.issue("gpsimd", fsc)
        sOUTg.n += 16

        emit_rb_all(p, c, fb, pp)

        def frb(e, _t=t):
            e.dma_start(io.rbsc[_t], s3(fb.scr[:, 0:NE * 256], NE)).then_inc(sOUTg.h, 16)
        p.issue("gpsimd", frb)
        sOUTg.n += 16

    for pi in range(NPASS):
        p.new_chain()
        pc = p.prev_chain
        el = list(range(pi * EPP, (pi + 1) * EPP))
        with_shared = (pi == 0) if phase == "p0" else (pi == NPASS - 1)
        last_pass = (pi == NPASS - 1)

        wc = io.wcache if io.wcache is not None else io.wcache_w

        def fwl(e, _el=tuple(el), _ws=with_shared, _pi=pi, _pc=pc,
                _sg=sOUTg.n, _ss=sOUTs.n, _wc=wc,
                _cr=cache_ready if (io.cast_weights and pi == 0) else 0):
            if _pc is not None and _pc[1] > 0:
                e.wait_ge(_pc[0].h, _pc[1])
            if _cr > 0:
                for _s in sWcw:
                    e.wait_ge(_s.h, _cr)
            if _sg > 0:
                e.wait_ge(sOUTg.h, _sg)
            if _ss > 0:
                e.wait_ge(sOUTs.h, _ss)
            for sl, eid in enumerate(_el):
                e.dma_start(fb.wg[sl][:], _wc[0][eid]).then_inc(sW.h, 16)
                e.dma_start(fb.wu[sl][:], _wc[1][eid]).then_inc(sW.h, 16)
                e.dma_start(fb.wd[sl][:], _wc[2][eid]).then_inc(sW.h, 16)
            if _ws:
                e.dma_start(fb.wg[4][:], _wc[0][NE]).then_inc(sW.h, 16)
                e.dma_start(fb.wu[4][:], _wc[1][NE]).then_inc(sW.h, 16)
                e.dma_start(fb.wd[4][:], _wc[2][NE]).then_inc(sW.h, 16)
            if phase == "p0" and _pi == NPASS - 1:
                e.dma_start(s3(fb.wg[4], KD), w_ap(c.wq)).then_inc(sW.h, 16)
                e.dma_start(s3(fb.wu[4], KD), w_ap(c.wk)).then_inc(sW.h, 16)
                e.dma_start(s3(fb.wd[4], KD), w_ap(c.wv)).then_inc(sW.h, 16)
        p.th["gpsimd"].append(fwl)
        nw = 3 * len(el) + (3 if with_shared else 0)
        if phase == "p0" and pi == NPASS - 1:
            nw += 3
        sW.n += 16 * nw
        wload_tgt = sW.n

        in_tgt = []

        def load_tile(t, e0=el[0], _pc0=pc):
            par = t % 2
            pcw = _pc0 if t == 0 else None
            outw = (sOUTg.n, sOUTs.n) if t == 0 else None

            def fls(e, _t=t, _par=par, _pc=pcw, _ow=outw):
                if _pc is not None and _pc[1] > 0:
                    e.wait_ge(_pc[0].h, _pc[1])
                if _ow is not None:
                    if _ow[0] > 0:
                        e.wait_ge(sOUTg.h, _ow[0])
                    if _ow[1] > 0:
                        e.wait_ge(sOUTs.h, _ow[1])
                e.dma_start(fb.x_in[_par][:],
                            io.x1[:, _t * 2048:(_t + 1) * 2048]
                            ).then_inc(sINs[_par].h, 16)
            p.issue("sync", fls)
            sINs[par].n += 16

            def flg(e, _t=t, _par=par, _e0=e0, _pc=pcw, _ow=outw):
                if _pc is not None and _pc[1] > 0:
                    e.wait_ge(_pc[0].h, _pc[1])
                if _ow is not None:
                    if _ow[0] > 0:
                        e.wait_ge(sOUTg.h, _ow[0])
                    if _ow[1] > 0:
                        e.wait_ge(sOUTs.h, _ow[1])
                e.dma_start(fb.ln_b[_par][:],
                            io.lnsc[:, _t * 2048:(_t + 1) * 2048]
                            ).then_inc(sINg[_par].h, 16)
                e.dma_start(s3(fb.rbs[_par][:, 0:EPP * 256], EPP),
                            io.rbsc_sl(_t, _e0)).then_inc(sINg[_par].h, 16)
            p.issue("scalar", flg)
            sINg[par].n += 32
            in_tgt.append((sINs[par].n, sINg[par].n))

        load_tile(0)
        for t in range(NT):
            par = t % 2
            if t + 1 < NT:
                load_tile(t + 1)
            ws, wg_ = in_tgt[t]
            ew = [(sINg[par], wg_)]
            if t == 0:
                ew.append((sW, wload_tgt))
            if last_pass and phase == "p0":
                ew.append((p.sq, p.sq.n))
            dn_w = (p.sq, p.sq.n)
            si_w = [(sOUTg, sOUTg.n)]
            h_w = [(sOUTg, sOUTg.n)]

            slots = [(fb.wg[sl], fb.wu[sl], fb.wd[sl], rb_bcast(fb.rbs[par], sl))
                     for sl in range(EPP)]
            if with_shared:
                slots.append((fb.wg[4], fb.wu[4], fb.wd[4], None))
            emit_experts(p, c, fb, pp, slots, es, fb.ln_b[par],
                         first_waits=ew, dn_wait=dn_w,
                         si_first_waits=si_w, h_first_waits=h_w)

            p.step("vector", lambda e, _par=par:
                   e.tensor_add(fb.xout[_par][:], fb.x_in[_par][:, 0:2048], pp.yacc[:]),
                   extra_waits=[(es.dn, es.dn.n), (sINs[par], ws),
                                (sOUTs, sOUTs.n)])

            if not (last_pass and phase == "p2"):
                def fw(e, _t=t, _par=par):
                    e.dma_start(io.x1[:, _t * 2048:(_t + 1) * 2048],
                                fb.xout[_par][:]).then_inc(sOUTs.h, 16)
                p.issue("sync", fw)
                sOUTs.n += 16

            if last_pass and phase == "p0":
                emit_ln(p, c, fb, pp, fb.xout[par], fb.ln_b[par])

                def fq(e, _par=par):
                    dsts = [pp.g, pp.u, pp.yacc]
                    for j in range(3):
                        srcw = [fb.wg[4], fb.wu[4], fb.wd[4]][j]
                        for m in range(KF):
                            for k in range(KD):
                                last = e.matmul(
                                    dsts[j][:, m * TT:(m + 1) * TT],
                                    srcw[:, k * DFF + m * 128: k * DFF + m * 128 + 128],
                                    fb.ln_b[_par][:, k * TT:(k + 1) * TT],
                                    start=(k == 0), stop=(k == KD - 1))
                    return last
                p.step("tensor", fq)

                def fqe(e):
                    e.tensor_copy(fb.t1, pp.g[:])
                    e.tensor_copy(fb.t2, pp.u[:])
                    return e.tensor_copy(fb.hh[0][:], pp.yacc[:, 0:1024])
                p.step("vector", fqe)

                def fqw(e, _t=t):
                    io.qkv_out(e, _t, sOUTg)
                p.issue("gpsimd", fqw)
                sOUTg.n += 16 * io.qkv_ndma

            if last_pass and phase == "p2":
                def ftr2(e, _par=par):
                    for s in range(2):
                        for fc in range(KD):
                            last = e.transpose(
                                pp.yacc[:, (s * KD + fc) * 128:(s * KD + fc) * 128 + 128],
                                fb.xout[_par][:, fc * 256 + s * 128: fc * 256 + s * 128 + 128],
                                c.id_f)
                    return last
                p.step("tensor", ftr2)
                p.step("vector", lambda e: e.tensor_copy(fb.x_fm[:], pp.yacc[:]))

                def fy(e, _t=t):
                    return e.dma_start(
                        d3(io.y[_t * TT:(_t + 1) * TT, :]),
                        s3(fb.x_fm, 2))
                p.step("sync", fy, dma=True)

    stk_ph.close()

    def fdr(e, _g=sOUTg.n, _s=sOUTs.n,
            _qg=p.qsem("gpsimd").n, _qs=p.qsem("sync").n):
        if _g:
            e.wait_ge(sOUTg.h, _g)
        if _s:
            e.wait_ge(sOUTs.h, _s)
        if _qg:
            e.wait_ge(p.qsem("gpsimd").h, _qg)
        if _qs:
            e.wait_ge(p.qsem("sync").h, _qs)
        return e.nop()
    p.step("gpsimd", fdr)
    return sOUTg


NSET = 3


class ScanBufs:
    def __init__(self, nc, stk):
        A = lambda n, s, d: stk.enter_context(nc.sbuf_tensor(uniq(n), s, d)).ap()
        self.qc = [A(f"qc{i}", [128, 768], BF16) for i in range(2)]
        self.qf = [A(f"qf{i}", [32, 3072], BF16) for i in range(2)]
        self.tok = A("tok", [128, 768], BF16)
        self.q5 = A("q5", [32, 1024], BF16)
        self.kw = A("kw", [128, 256], BF16)
        self.qw = A("qw", [128, 256], BF16)
        self.scg = A("scg", [32, 768], F32)
        self.scgb = A("scgb", [32, 768], BF16)
        self.aw = [A(f"awsc{i}", [128, 448], BF16) for i in range(NSET)]
        self.bm = [A(f"bm{i}", [128, 256], BF16) for i in range(NSET)]
        self.fm = [A(f"fmsc{i}", [128, 128], BF16) for i in range(NSET)]
        self.dm = [A(f"dmsc{i}", [128, 128], BF16) for i in range(NSET)]
        self.kcn = [A(f"kcn{i}", [128, 32], BF16) for i in range(NSET)]
        self.obt = [A(f"obt{i}", [128, 32], F32) for i in range(NSET)]
        self.osb = A("osb", [128, 256], BF16)
        self.ofm = A("ofm", [128, 256], BF16)
        self.gq5b = A("gq5b", [32, 128], F32)


class ScanPsum:
    def __init__(self, nc, stk):
        E = stk.enter_context
        self.aq = [E(nc.psum_tensor(uniq("saq"), [128, 512], F32)).ap()
                   for _ in range(NSET)]
        self.fdq = [E(nc.psum_tensor(uniq("sfdq"), [128, 512], F32)).ap()
                    for _ in range(NSET)]
        tb = E(nc.psum_tensor(uniq("stok"), [128, 1024], BF16)).ap()
        self.tokp = tb[:, 0:768]
        self.ofmp = tb[:, 768:1024]


class SSems:
    def __init__(self, nc):
        for n in ("t1", "s2", "v3", "t4", "v5", "t6", "v7", "t8", "v10"):
            setattr(self, n, Sem(nc, f"sc_{n}"))


def emit_scan(p, c, io):
    nc = p.nc
    p.new_chain()
    pc = p.prev_chain
    sX = [Sem(nc, "sX_scan0"), Sem(nc, "sX_scan1")]
    sOUT = Sem(nc, "sOUT_scan")
    ss = SSems(nc)
    with contextlib.ExitStack() as stk:
        sb = ScanBufs(nc, stk)
        ps = ScanPsum(nc, stk)

        def fz(e, _pc=pc):
            if _pc is not None and _pc[1] > 0:
                e.wait_ge(_pc[0].h, _pc[1])
            e.memset(sb.scg[:], 0.0)
            for s in range(NSET):
                e.memset(ps.aq[s][:], 0.0)
                e.memset(ps.fdq[s][:], 0.0)
            return e.memset(sb.gq5b[:], 0.0)
        p.step("vector", fz, extra_waits=[(io.sC, io.sC.n)])
        p.step("vector", lambda e: e.tensor_copy(sb.gq5b[:], c.gq5[0:32, :]))

        gpair = 0
        for ch in range(NCH):
            par = ch % 2

            def fqd(e, _ch=ch, _par=par, _t8=ss.t8.n):
                if _t8 > 0:
                    e.wait_ge(ss.t8.h, _t8)
                io.qkv_in(e, _ch, sb.qc[_par], sb.qf[_par], sX[_par])
            p.issue("gpsimd", fqd, chain_wait=False)
            sX[par].n += 32

            def fcast(e, _v10=ss.v10.n, _o=sOUT.n):
                e.wait_ge(ss.v10.h, _v10) if _v10 > 0 else None
                if _o > 0:
                    e.wait_ge(sOUT.h, _o)
                return e.tensor_copy(sb.scgb[:], sb.scg[:])
            p.step("vector", fcast, extra_waits=[(sX[par], sX[par].n)])

            qs, qf = sb.qc[par], sb.qf[par]

            def ftr(e, _qs=qs):
                for s in range(6):
                    last = e.transpose(ps.tokp[:, s * 128:(s + 1) * 128],
                                       _qs[:, s * 128:(s + 1) * 128], c.id_b)
                return last
            p.step("tensor", ftr)
            p.step("scalar", lambda e: e.copy(sb.tok[:], ps.tokp[:]))

            def fsc(e, _qf=qf):
                for pr8 in range(8):
                    e.tensor_mul(sb.q5[:, pr8 * 128:(pr8 + 1) * 128],
                                 _qf[:, pr8 * 128:(pr8 + 1) * 128], sb.gq5b[:])
                e.tensor_scalar_mul(sb.kw[:], sb.tok[:, 256:512], c.gcols[:, 2:3])
                return e.tensor_scalar_mul(sb.qw[:], sb.tok[:, 0:256], c.gcols[:, 2:3])
            p.step("vector", fsc)
            head = (p.sq, p.sq.n)

            stA, stB, stC, stD = [], [], [], []
            svS, svV = [], []
            for pr in range(8):
                g = gpair + pr
                st = g % NSET
                aq, fdq = ps.aq[st], ps.fdq[st]
                aw, bm, fm, dm, kcn, obt = (sb.aw[st], sb.bm[st], sb.fm[st],
                                            sb.dm[st], sb.kcn[st], sb.obt[st])
                s_, r = pr // 4, (pr % 4) * 32
                qff = qf[:, pr * 128:(pr + 1) * 128]
                kff = qf[:, (8 + pr) * 128:(9 + pr) * 128]
                q5f = sb.q5[:, pr * 128:(pr + 1) * 128]
                ktk = sb.tok[:, 256 + s_ * 128 + r: 256 + s_ * 128 + r + 32]
                vtk = sb.tok[:, 512 + s_ * 128 + r: 512 + s_ * 128 + r + 32]
                kwk = sb.kw[:, s_ * 128 + r: s_ * 128 + r + 32]
                qwk = sb.qw[:, s_ * 128 + r: s_ * 128 + r + 32]
                Ssl = sb.scgb[:, pr * 32:(pr + 1) * 32]
                Csl = sb.scgb[:, 256 + pr * 32: 256 + (pr + 1) * 32]
                Gsl = sb.scgb[:, 512 + pr * 32: 512 + (pr + 1) * 32]

                def f1(e, _q=qff, _k=kff, _S=Ssl, _C=Csl, _aq=aq, _g=g,
                       _head=head):
                    e.wait_ge(_head[0].h, _head[1])
                    if _g - (NSET - 1) > 0:
                        e.wait_ge(ss.v10.h, _g - (NSET - 1))
                    e.matmul(_aq[:, 0:128], _q, _k, start=True, stop=True)
                    e.matmul(_aq[:, 128:256], _k, _q, start=True, stop=True)
                    e.matmul(_aq[0:32, 256:384], _S, _q, start=True, stop=True)
                    e.matmul(_aq[0:32, 384:416], _S, _C, start=True, stop=True)
                    return e.matmul(_aq[:, 416:448], _k, _C,
                                    start=True, stop=True).then_inc(ss.t1.h, 1)
                stA.append(f1)

                def f2(e, _aq=aq, _aw=aw, _g=g):
                    e.wait_ge(ss.t1.h, _g + 1)
                    return e.copy(_aw[:], _aq[:, 0:448]).then_inc(ss.s2.h, 1)
                svS.append(f2)

                def f3(e, _aw=aw, _bm=bm, _g=g):
                    e.wait_ge(ss.s2.h, _g + 1)
                    return e.tensor_mul(_bm[:], _aw[:, 0:256],
                                        c.maskp[:]).then_inc(ss.v3.h, 1)
                v3 = f3

                def f4(e, _q=qff, _aw=aw, _fdq=fdq, _g=g):
                    e.wait_ge(ss.v3.h, _g + 1)
                    return e.matmul(_fdq[:, 0:128], _q, _aw[0:32, 256:384],
                                    start=True, stop=True).then_inc(ss.t4.h, 1)
                stB.append(f4)

                def f5(e, _fdq=fdq, _fm=fm, _g=g):
                    e.wait_ge(ss.t4.h, _g + 1)
                    return e.tensor_mul(_fm[:], _fdq[:, 0:128],
                                        c.maskp[:, 128:256]).then_inc(ss.v5.h, 1)
                v5 = f5

                def f6(e, _aw=aw, _bm=bm, _v=vtk, _aq=aq, _fdq=fdq, _g=g):
                    e.wait_ge(ss.v5.h, _g + 1)
                    e.matmul(_fdq[:, 128:256], _aw[:, 128:256], _bm[:, 128:256],
                             start=True, stop=True)
                    return e.matmul(_aq[:, 448:480], _bm[:, 0:128], _v,
                                    start=True, stop=True).then_inc(ss.t6.h, 1)
                stC.append(f6)

                def f7(e, _fdq=fdq, _dm=dm, _aw=aw, _kcn=kcn, _aq=aq, _g=g):
                    e.wait_ge(ss.t6.h, _g + 1)
                    e.tensor_mul(_dm[:], _fdq[:, 128:256], c.maskp[:, 128:256])
                    return e.scalar_tensor_tensor(
                        _kcn[:], _aw[:, 416:448], c.gcols[:, 1:2],
                        _aq[:, 448:480], ALU.mult, ALU.subtract
                    ).then_inc(ss.v7.h, 1)
                v7 = f7

                def f8(e, _q5=q5f, _q=qff, _G=Gsl, _v=vtk, _aw=aw, _bm=bm,
                       _fm=fm, _dm=dm, _kcn=kcn, _fdq=fdq, _kw=kwk, _qw=qwk,
                       _k=ktk, _g=g):
                    e.wait_ge(ss.v7.h, _g + 1)
                    e.matmul(_fdq[:, 256:288], _q5, _aw[0:32, 384:416],
                             start=True, stop=False)
                    e.matmul(_fdq[:, 256:288], _q, _G, start=False, stop=False)
                    e.matmul(_fdq[:, 256:288], _fm[:], _v, start=False, stop=False)
                    e.matmul(_fdq[:, 256:288], _bm[:, 128:256], _aw[:, 416:448],
                             start=False, stop=True)
                    e.matmul(_fdq[:, 288:320], _dm[:], _v, start=True, stop=False)
                    e.matmul(_fdq[:, 288:320], _bm[:, 128:256], _kcn[:],
                             start=False, stop=True)
                    e.matmul(_fdq[0:32, 320:352], _kw, _k, start=True, stop=True)
                    e.matmul(_fdq[0:32, 352:384], _qw, _v, start=True, stop=True)
                    return e.matmul(_fdq[0:32, 384:416], _kw, _kcn[:],
                                    start=True, stop=True).then_inc(ss.t8.h, 1)
                stD.append(f8)

                def f10(e, _fdq=fdq, _obt=obt, _pr=pr, _g=g):
                    e.wait_ge(ss.t8.h, _g + 1)
                    e.tensor_scalar_mul(_obt[:], _fdq[:, 256:288], c.gcols[:, 0:1])
                    e.drain()
                    e.tensor_add(sb.osb[:, _pr * 32:(_pr + 1) * 32],
                                 _obt[:], _fdq[:, 288:320])
                    Sf = sb.scg[:, _pr * 32:(_pr + 1) * 32]
                    Cf = sb.scg[:, 256 + _pr * 32: 256 + (_pr + 1) * 32]
                    Gf = sb.scg[:, 512 + _pr * 32: 512 + (_pr + 1) * 32]
                    e.scalar_tensor_tensor(Sf, Sf, GL, _fdq[0:32, 320:352],
                                           ALU.mult, ALU.add)
                    e.scalar_tensor_tensor(Cf, Cf, GL, _fdq[0:32, 352:384],
                                           ALU.mult, ALU.add)
                    return e.scalar_tensor_tensor(
                        Gf, Gf, GL, _fdq[0:32, 384:416],
                        ALU.mult, ALU.add).then_inc(ss.v10.h, 1)
                svV.append((v3, v5, v7, f10))

            ss.t1.n += 8
            ss.s2.n += 8
            ss.v3.n += 8
            ss.t4.n += 8
            ss.v5.n += 8
            ss.t6.n += 8
            ss.v7.n += 8
            ss.t8.n += 8
            ss.v10.n += 8
            gpair += 8

            for s in range(8 + 3):
                if 3 <= s and s - 3 < 8:
                    p.raw("tensor", stD[s - 3])
                    p.raw("vector", svV[s - 3][3])
                if 2 <= s and s - 2 < 8:
                    p.raw("tensor", stC[s - 2])
                    p.raw("vector", svV[s - 2][2])
                if 1 <= s and s - 1 < 8:
                    p.raw("tensor", stB[s - 1])
                    p.raw("vector", svV[s - 1][1])
                if s < 8:
                    p.raw("tensor", stA[s])
                    p.raw("scalar", svS[s])
                    p.raw("vector", svV[s][0])

            def fot(e, _v10=ss.v10.n):
                e.wait_ge(ss.v10.h, _v10)
                e.transpose(ps.ofmp[:, 0:128], sb.osb[:, 0:128], c.id_b)
                return e.transpose(ps.ofmp[:, 128:256], sb.osb[:, 128:256], c.id_b)
            p.step("tensor", fot)
            p.step("scalar", lambda e: e.copy(sb.ofm[:], ps.ofmp[:]))

            def fow(e, _ch=ch):
                io.o_out(e, _ch, sb.ofm, sOUT)
            p.issue("gpsimd", fow)
            sOUT.n += 16 * io.o_ndma

        def fdr(e, _n=sOUT.n):
            if _n:
                e.wait_ge(sOUT.h, _n)
            return e.nop()
        p.step("gpsimd", fdr)
    return sOUT


def build_p0_program():
    nc = make_nc()
    c = Ctx()
    decl_common(nc, c)
    io = Ctx()
    io.x = nc.dram_tensor("x", [TOKC, D], F32, kind="ExternalInput").ap()
    io.x1 = nc.dram_tensor("x1", [128, NT * 2048], F32, kind="ExternalOutput").ap()
    io.lnsc = nc.dram_tensor("lnsc", [128, NT * 2048], BF16, kind="Internal").ap()
    rb_scr = nc.dram_tensor("rb_scr", [NT, NE, 128, 256], BF16, kind="Internal").ap()
    io.rbsc = [rb_scr[t].transpose([1, 0, 2]) for t in range(NT)]
    io.rbsc_sl = lambda t, e0: rb_scr[t, e0:e0 + EPP].transpose([1, 0, 2])
    io.qkvd = nc.dram_tensor("qkv", [3, DFF, TOKC], BF16, kind="ExternalOutput").ap()
    wc_g = nc.dram_tensor("wc_g", [NE + 1, 128, WIMG], BF16, kind="ExternalOutput").ap()
    wc_u = nc.dram_tensor("wc_u", [NE + 1, 128, WIMG], BF16, kind="ExternalOutput").ap()
    wc_d = nc.dram_tensor("wc_d", [NE + 1, 128, WIMGD], BF16, kind="ExternalOutput").ap()
    io.wcache = None
    io.wcache_w = ([wc_g[i] for i in range(NE + 1)],
                   [wc_u[i] for i in range(NE + 1)],
                   [wc_d[i] for i in range(NE + 1)])
    io.cast_weights = True

    def qkv_out(e, t, sem, _io=io):
        for j, src in enumerate([io._fb.t1, io._fb.t2, io._fb.hh[0]]):
            e.dma_start(
                d3(_io.qkvd[j, :, t * TT:(t + 1) * TT]),
                s3(src, KF)).then_inc(sem.h, 16)
    io.qkv_out = qkv_out
    io.qkv_ndma = 3

    p = Prog(nc)
    sC = Sem(nc, "sC")
    io.sC = sC
    load_consts(p, c, sC)
    with contextlib.ExitStack() as stk:
        fb = FfnBufs(nc, stk)
        io._fb = fb
        emit_ffn_phase(p, c, fb, io, "p0")
        p.emit("p0")
    return nc


def build_scan_program():
    nc = make_nc()
    c = Ctx()
    decl_common(nc, c)
    io = Ctx()
    io.qkv_me = nc.dram_tensor("qkv_me", [NCH, 768, L], BF16, kind="ExternalInput").ap()
    io.o_me = nc.dram_tensor("o_me", [NCH, 2 * L, L], BF16, kind="ExternalOutput").ap()

    def qkv_in(e, ch, qc, qf, sem, _io=io):
        e.dma_start(s3(qc, 6), d3(_io.qkv_me[ch])).then_inc(sem.h, 16)
        e.dma_start(s3(qf, 24), d3(_io.qkv_me[ch], p=32)).then_inc(sem.h, 16)
    io.qkv_in = qkv_in

    def o_out(e, ch, ofm, sem, _io=io):
        e.dma_start(_io.o_me[ch, 0:128, :], ofm[:, 0:128]).then_inc(sem.h, 16)
        e.dma_start(_io.o_me[ch, 128:256, :], ofm[:, 128:256]).then_inc(sem.h, 16)
    io.o_out = o_out
    io.o_ndma = 2

    p = Prog(nc)
    sC = Sem(nc, "sC")
    io.sC = sC
    load_consts(p, c, sC)
    emit_scan(p, c, io)
    p.emit("scan")
    return nc


def build_p2_program():
    nc = make_nc()
    c = Ctx()
    decl_common(nc, c)
    io = Ctx()
    io.x1 = nc.dram_tensor("x1", [128, NT * 2048], F32, kind="ExternalInput").ap()
    io.o_mine = nc.dram_tensor("o_mine", [DFF, TOKC], BF16, kind="ExternalInput").ap()
    io.y = nc.dram_tensor("y", [TOKC, D], F32, kind="ExternalOutput").ap()
    io.lnsc = nc.dram_tensor("lnsc", [128, NT * 2048], BF16, kind="Internal").ap()
    rb_scr = nc.dram_tensor("rb_scr", [NT, NE, 128, 256], BF16, kind="Internal").ap()
    io.rbsc = [rb_scr[t].transpose([1, 0, 2]) for t in range(NT)]
    io.rbsc_sl = lambda t, e0: rb_scr[t, e0:e0 + EPP].transpose([1, 0, 2])
    wc_g = nc.dram_tensor("wc_g", [NE + 1, 128, WIMG], BF16, kind="ExternalInput").ap()
    wc_u = nc.dram_tensor("wc_u", [NE + 1, 128, WIMG], BF16, kind="ExternalInput").ap()
    wc_d = nc.dram_tensor("wc_d", [NE + 1, 128, WIMGD], BF16, kind="ExternalInput").ap()
    io.wcache = ([wc_g[i] for i in range(NE + 1)],
                 [wc_u[i] for i in range(NE + 1)],
                 [wc_d[i] for i in range(NE + 1)])
    io.cast_weights = False

    def o_load(e, t, _io=io):
        return e.dma_start(
            s3(_io._fb.o_in, KF),
            d3(_io.o_mine[:, t * TT:(t + 1) * TT]))
    io.o_load = o_load

    p = Prog(nc)
    sC = Sem(nc, "sC")
    io.sC = sC
    load_consts(p, c, sC)
    with contextlib.ExitStack() as stk:
        fb = FfnBufs(nc, stk)
        io._fb = fb
        emit_ffn_phase(p, c, fb, io, "p2")
        p.emit("p2")
    return nc


def kernel(L0_attn_x, L0_attn_Wq, L0_attn_Wk, L0_attn_Wv, L0_attn_Wo,
           L0_ffn_gate_w, L0_ffn_w_gate, L0_ffn_w_up, L0_ffn_w_down,
           L0_ffn_ws_gate, L0_ffn_ws_up, L0_ffn_ws_down):
    x = np.ascontiguousarray(np.asarray(L0_attn_x, np.float32))
    consts = host_consts()
    wmap = dict(
        wq=np.ascontiguousarray(L0_attn_Wq, np.float32),
        wk=np.ascontiguousarray(L0_attn_Wk, np.float32),
        wv=np.ascontiguousarray(L0_attn_Wv, np.float32),
        wo=np.ascontiguousarray(L0_attn_Wo, np.float32),
        gate_wT=np.ascontiguousarray(np.asarray(L0_ffn_gate_w, np.float32).T),
        we_gate=np.ascontiguousarray(L0_ffn_w_gate, np.float32),
        we_up=np.ascontiguousarray(L0_ffn_w_up, np.float32),
        we_down=np.ascontiguousarray(L0_ffn_w_down, np.float32),
        ws_gate=np.ascontiguousarray(L0_ffn_ws_gate, np.float32),
        ws_up=np.ascontiguousarray(L0_ffn_ws_up, np.float32),
        ws_down=np.ascontiguousarray(L0_ffn_ws_down, np.float32),
        **consts,
    )

    if FUSED:
        return kernel_fused(wmap, x)

    nc1 = build_p0_program()
    in1 = []
    for cid in range(NCORE):
        b, h = cid // 2, cid % 2
        m = dict(wmap)
        m["x"] = np.ascontiguousarray(x[b, h * TOKC:(h + 1) * TOKC, :])
        in1.append(m)
    r1 = run_bass_kernel_spmd(nc1, in1, core_ids=list(range(NCORE)))
    x1 = [r1.results[cid]["x1"] for cid in range(NCORE)]
    qkv = [r1.results[cid]["qkv"] for cid in range(NCORE)]
    wcg = r1.results[0]["wc_g"]
    wcu = r1.results[0]["wc_u"]
    wcd = r1.results[0]["wc_d"]

    nc2 = build_scan_program()
    in2 = []
    for cid in range(NCORE):
        b, hh = cid // 2, cid % 2
        qa, qb = qkv[2 * b], qkv[2 * b + 1]
        rows = slice(hh * 256, (hh + 1) * 256)
        me = np.empty((NCH, 768, L), qa.dtype)
        for j in range(3):
            full = np.concatenate([qa[j][rows], qb[j][rows]], axis=1)
            me[:, j * 256:(j + 1) * 256, :] = (
                full.reshape(256, NCH, L).transpose(1, 0, 2))
        in2.append({k: wmap[k] for k in wmap} | {"qkv_me": me})
    r2 = run_bass_kernel_spmd(nc2, in2, core_ids=list(range(NCORE)))

    nc3 = build_p2_program()
    in3 = []
    for cid in range(NCORE):
        b, h = cid // 2, cid % 2
        o0 = r2.results[2 * b]["o_me"]
        o1 = r2.results[2 * b + 1]["o_me"]
        ofull = np.concatenate([
            o0.transpose(1, 0, 2).reshape(256, T),
            o1.transpose(1, 0, 2).reshape(256, T)], axis=0)
        m = dict(wmap)
        m["x1"] = x1[cid]
        m["o_mine"] = np.ascontiguousarray(ofull[:, h * TOKC:(h + 1) * TOKC])
        m["wc_g"], m["wc_u"], m["wc_d"] = wcg, wcu, wcd
        in3.append(m)
    r3 = run_bass_kernel_spmd(nc3, in3, core_ids=list(range(NCORE)))

    y = np.empty((B, T, D), np.float32)
    for cid in range(NCORE):
        b, h = cid // 2, cid % 2
        y[b, h * TOKC:(h + 1) * TOKC, :] = r3.results[cid]["y"]
    return y


class DualInc:
    def __init__(self, *insts):
        self.insts = insts

    def then_inc(self, sem, val):
        for i in self.insts:
            i.then_inc(sem, val)
        return self


def d3dyn(x3, p=128):
    return x3.rearrange("b (a p) t -> b a p t", p=p).transpose([0, 2, 1, 3])


def build_fused_program():
    nc = make_nc()
    c = Ctx()
    decl_common(nc, c)

    x_in = nc.dram_tensor("x", [TOKC, D], F32, kind="ExternalInput").ap()
    y_out = nc.dram_tensor("y", [TOKC, D], F32, kind="ExternalOutput").ap()
    x1_scr = nc.dram_tensor("x1_scr", [128, NT * 2048], F32, kind="Internal").ap()
    lnsc = nc.dram_tensor("lnsc_s", [128, NT * 2048], BF16, kind="Internal").ap()
    rb_scr = nc.dram_tensor("rb_scr_s", [NT, NE, 128, 256], BF16, kind="Internal").ap()
    rbsc = [rb_scr[t].transpose([1, 0, 2]) for t in range(NT)]
    rbsc_sl = lambda t, e0: rb_scr[t, e0:e0 + EPP].transpose([1, 0, 2])
    wc_g = nc.dram_tensor("wc_g", [NE + 1, 128, WIMG], BF16, kind="Internal").ap()
    wc_u = nc.dram_tensor("wc_u", [NE + 1, 128, WIMG], BF16, kind="Internal").ap()
    wc_d = nc.dram_tensor("wc_d", [NE + 1, 128, WIMGD], BF16, kind="Internal").ap()
    wcache = ([wc_g[i] for i in range(NE + 1)],
              [wc_u[i] for i in range(NE + 1)],
              [wc_d[i] for i in range(NE + 1)])
    shm_qkv = nc.dram_tensor("shm_qkv", [2, NCH, 768, L], BF16,
                             kind="Internal", addr_space="Shared").ap()
    shm_o = nc.dram_tensor("shm_o", [DFF, T], BF16,
                           kind="Internal", addr_space="Shared").ap()

    p = Prog(nc)
    sC = Sem(nc, "sC")

    io0 = Ctx()
    io0.x = x_in
    io0.x1 = x1_scr
    io0.lnsc = lnsc
    io0.rbsc, io0.rbsc_sl = rbsc, rbsc_sl
    io0.wcache = None
    io0.wcache_w = wcache
    io0.cast_weights = True
    io0.sC = sC

    _hh_regs = {}

    def hh_reg(e):
        if id(e) not in _hh_regs:
            _hh_regs[id(e)] = e.to_reg(e.partition_id() % 2)
        return _hh_regs[id(e)]

    def qkv_out(e, t, sem, _fb=None):
        fb = io0._fb

        def arm(hh):
            for j, srcb in enumerate([fb.t1, fb.t2, fb.hh[0]]):
                for m in range(KF):
                    for ch in range(2):
                        e.dma_start(
                            shm_qkv[m // 2, hh * 32 + 2 * t + ch,
                                    j * 256 + (m % 2) * 128: j * 256 + (m % 2) * 128 + 128, :],
                            srcb[:, m * 256 + ch * 128: m * 256 + ch * 128 + 128]
                        ).then_inc(sem.h, 16)
        r = hh_reg(e)
        with e.If_eq(r, 0):
            arm(0)
        with e.Else():
            arm(1)
    io0.qkv_out = qkv_out
    io0.qkv_ndma = 24

    load_consts(p, c, sC)
    with contextlib.ExitStack() as stk:
        fb = FfnBufs(nc, stk)
        io0._fb = fb
        emit_ffn_phase(p, c, fb, io0, "p0")
        p.emit("p0")
    nc.all_core_barrier()

    ios = Ctx()
    ios.sC = sC

    def qkv_in(e, ch, qc, qf, sem):
        r = hh_reg(e)

        def arm(hh):
            blk = shm_qkv[hh, ch]
            e.dma_start(s3(qc, 6), d3(blk)).then_inc(sem.h, 16)
            e.dma_start(s3(qf, 24), d3(blk, p=32)).then_inc(sem.h, 16)
        with e.If_eq(r, 0):
            arm(0)
        with e.Else():
            arm(1)
    ios.qkv_in = qkv_in

    def o_out(e, ch, ofm, sem):
        r = hh_reg(e)

        def arm(hh):
            for s in range(2):
                e.dma_start(
                    shm_o[hh * 256 + s * 128: hh * 256 + (s + 1) * 128,
                          ch * L:(ch + 1) * L],
                    ofm[:, s * 128:(s + 1) * 128]).then_inc(sem.h, 16)
        with e.If_eq(r, 0):
            arm(0)
        with e.Else():
            arm(1)
    ios.o_out = o_out
    ios.o_ndma = 2

    emit_scan(p, c, ios)
    p.emit("scan")
    nc.all_core_barrier()

    io2 = Ctx()
    io2.x1 = x1_scr
    io2.lnsc = lnsc
    io2.rbsc, io2.rbsc_sl = rbsc, rbsc_sl
    io2.y = y_out
    io2.wcache = wcache
    io2.cast_weights = False
    io2.sC = sC

    def o_load(e, t):
        fb2 = io2._fb
        r = hh_reg(e)
        with e.If_eq(r, 0):
            i1 = e.dma_start(s3(fb2.o_in, KF),
                             d3(shm_o[:, t * TT:(t + 1) * TT]))
        with e.Else():
            i2 = e.dma_start(s3(fb2.o_in, KF),
                             d3(shm_o[:, TOKC + t * TT: TOKC + (t + 1) * TT]))
        return DualInc(i1, i2)
    io2.o_load = o_load

    with contextlib.ExitStack() as stk:
        fb2 = FfnBufs(nc, stk)
        io2._fb = fb2
        emit_ffn_phase(p, c, fb2, io2, "p2")
        p.emit("p2")
    return nc


FUSED = True


def kernel_fused(wmap, x):
    nc = build_fused_program()
    ins = []
    for cid in range(NCORE):
        b, h = cid // 2, cid % 2
        m = dict(wmap)
        m["x"] = np.ascontiguousarray(x[b, h * TOKC:(h + 1) * TOKC, :])
        ins.append(m)
    res = run_bass_kernel_spmd(nc, ins, core_ids=list(range(NCORE)))
    y = np.empty((B, T, D), np.float32)
    for cid in range(NCORE):
        b, h = cid // 2, cid % 2
        y[b, h * TOKC:(h + 1) * TOKC, :] = res.results[cid]["y"]
    return y
